# version 36
# baseline (speedup 1.0000x reference)
import numpy as np
from contextlib import ExitStack

import concourse.bass as bass
import concourse.mybir as mybir
from concourse.bass_utils import run_bass_kernel_spmd

F32 = mybir.dt.float32
BF16 = mybir.dt.bfloat16
AF = mybir.ActivationFunctionType
ALU = mybir.AluOpType
AX = mybir.AxisListType

N = 4096
D = 1024
H = 8
DH = 64
INNER = H * DH
HID = 4096
TT = 1024
NPASS = N // TT
NTC = N // 128
EPS = 1e-5

TAPS = ()

ENGS = ("sync", "tensor", "scalar", "vector", "gpsimd")


class SemC:
    __slots__ = ("h", "n")

    def __init__(self, h):
        self.h = h
        self.n = 0


class Prog:
    def __init__(self, nc):
        self.nc = nc
        self.ops = {e: [] for e in ENGS}
        self.gidx = 0

    def do(self, eng, fn, waits=(), inc=None, dma=False):
        self.gidx += 1
        wl = []
        for w in waits:
            if w is None:
                continue
            s, v = w[0], w[1]
            if len(w) > 2:
                assert w[2] < self.gidx, "wait on a not-yet-emitted token"
            if v > 0:
                wl.append((s.h, v))
        k = 16 if dma else 1
        tok = None
        if inc is not None:
            inc.n += k
            tok = (inc, inc.n, self.gidx)
        self.ops[eng].append((fn, tuple(wl), (inc.h, k) if inc is not None else None))
        return tok

    def set_barrier_ctx(self, bsc, sems):
        self._bsc = bsc
        self._bar_sems = sems

    def barrier(self, bar=None):
        bsc = self._bsc
        self.do("vector", lambda e: e.memset(bsc[:, 0:1], 0.0), inc=self._sV)
        self.do("scalar", lambda e: e.memzero(bsc[:, 1:2]), inc=self._sS)
        self.do("gpsimd", lambda e: e.memset(bsc[:, 2:3], 0.0), inc=self._sG)
        snap = tuple((s, s.n) for s in self._bar_sems)
        for e in ENGS:
            self.do(e, None, waits=snap)

    def emit(self):
        with self.nc.Block() as block:
            for name in ENGS:
                def mk(n):
                    def body(eng):
                        for fn, waits, inc in self.ops[n]:
                            for h, v in waits:
                                eng.wait_ge(h, v)
                            inst = fn(eng) if fn is not None else None
                            if inc is not None:
                                if inst is None:
                                    inst = eng.nop()
                                inst.then_inc(inc[0], inc[1])
                    return body
                getattr(block, name)(mk(name))


def build(taps=(), phase_limit=3, n_ic=8, n_pair=4, n_pass=NPASS, ffn_stage=5, w3_mode=0):
    nc = bass.Bass("TRN2", target_bir_lowering=False)
    P = Prog(nc)

    x_d = nc.dram_tensor("x", [N, D], F32, kind="ExternalInput").ap()
    wq_d = nc.dram_tensor("wq", [D, INNER], F32, kind="ExternalInput").ap()
    wkv_d = nc.dram_tensor("wkv", [D, 2 * DH], F32, kind="ExternalInput").ap()
    wout_d = nc.dram_tensor("wout", [INNER, D], F32, kind="ExternalInput").ap()
    gq_d = nc.dram_tensor("gq", [D], F32, kind="ExternalInput").ap()
    go_d = nc.dram_tensor("go", [D], F32, kind="ExternalInput").ap()
    w12_d = nc.dram_tensor("w12", [32, 128, 16, 128], F32,
                           kind="ExternalInput").ap()
    h1_d = nc.dram_tensor("h1", [32, 128, 32, 128], F32, kind="ExternalInput").ap()
    h2_d = nc.dram_tensor("h2", [32, 128, 32, 128], F32, kind="ExternalInput").ap()
    w3_d = nc.dram_tensor("w3", [8, 128, 32, 128], F32, kind="ExternalInput").ap()
    s1_d = nc.dram_tensor("s1", [HID], F32, kind="ExternalInput").ap()
    s2_d = nc.dram_tensor("s2", [HID], F32, kind="ExternalInput").ap()
    sh1_d = nc.dram_tensor("sh1", [HID], F32, kind="ExternalInput").ap()
    sh2_d = nc.dram_tensor("sh2", [HID], F32, kind="ExternalInput").ap()
    s3_d = nc.dram_tensor("s3", [D], F32, kind="ExternalInput").ap()
    out_d = nc.dram_tensor("out", [N, D], F32, kind="ExternalOutput").ap()
    x2s_d = nc.dram_tensor("x2s", [N, D], F32,
                           kind="ExternalOutput" if "x2" in taps else "Internal").ap()
    tap_d = {}
    for t in taps:
        if t == "x2":
            continue
        shape = {"xn": [128, D], "qfm": [128, N], "kfm": [128, N],
                 "vones": [128, 32 * 65], "attn": [128, 4 * 512],
                 "hfm": [128, 32 * TT], "u1fm": [128, 32 * TT],
                 "ffntm": [128, 2048]}[t]
        tap_d[t] = nc.dram_tensor("tap_" + t, shape, F32, kind="ExternalOutput").ap()

    last_tap = [None]

    with ExitStack() as root:
        rsb = lambda name, shape, dt: root.enter_context(nc.sbuf_tensor(name, shape, dt))
        sem = lambda name: SemC(root.enter_context(nc.semaphore(name)))

        bar = sem("bar")
        dC = sem("dC")
        dWA = sem("dWA")
        dX = [sem("dX0"), sem("dX1")]
        dF = [sem("dF0"), sem("dF1")]
        dW1 = [sem("dW1a"), sem("dW1b")]
        dW2 = [sem("dW2a"), sem("dW2b")]
        dWH = [sem("dWHa"), sem("dWHb")]
        dS = sem("dS")
        dS2 = sem("dS2")
        dT = sem("dT")
        sT = sem("sT")
        sS = sem("sS")
        sV = sem("sV")
        sG = sem("sG")

        bsc = rsb("bsc", [128, 4], F32)
        PS_t = root.enter_context(nc.psum_tensor("PS", [128, 4096], F32))
        PSf = PS_t.ap()
        PSb = PS_t.ap().bitcast(BF16)
        P._sV, P._sS, P._sG = sV, sS, sG
        P.set_barrier_ctx(bsc, (sT, sS, sV, sG, dC, dWA, dX[0], dX[1],
                                dF[0], dF[1],
                                dW1[0], dW1[1], dW2[0], dW2[1],
                                dWH[0], dWH[1], dS, dS2))
        ident_bf = rsb("ident_bf", [128, 128], BF16)
        ident_f = rsb("ident_f", [32, 32], F32)
        consts = rsb("consts", [128, 12], F32)
        ones64 = rsb("ones64", [33, 64], BF16)
        s1c = rsb("s1c", [128, 32], F32)
        s2c = rsb("s2c", [128, 32], F32)
        sh1c = rsb("sh1c", [128, 32], F32)
        sh2c = rsb("sh2c", [128, 32], F32)
        s3c = rsb("s3c", [128, 8], F32)
        xb0 = [rsb("xb0a", [128, D], F32), rsb("xb0b", [128, D], F32)]
        st0 = rsb("st0", [128, 8], F32)
        fbf0 = rsb("fbf0", [128, 8 * D], BF16)

        neg16 = consts[:, 0:1]
        eps_t = consts[:, 1:2]
        clamp_q = consts[:, 2:3]
        clamp_k = consts[:, 3:4]
        eps2_t = consts[:, 4:5]
        magic4 = consts[:, 5:9]

        t_ms1 = P.do("gpsimd", lambda e: e.memset(ident_bf[:], 0.0), inc=sG)
        P.do("gpsimd", lambda e: e.affine_select(
            out=ident_bf[:], in_=ident_bf[:], compare_op=ALU.not_equal, fill=1.0,
            base=0, pattern=[[-1, 128]], channel_multiplier=1), waits=(t_ms1,),
            inc=sG)
        t_ms2 = P.do("gpsimd", lambda e: e.memset(ident_f[:], 0.0), inc=sG)
        tok_idf = P.do("gpsimd", lambda e: e.affine_select(
            out=ident_f[:], in_=ident_f[:], compare_op=ALU.not_equal, fill=1.0,
            base=0, pattern=[[-1, 32]], channel_multiplier=1), waits=(t_ms2,),
            inc=sG)
        P.do("gpsimd", lambda e: e.memset(consts[:, 0:1], -16.0), inc=sG)
        P.do("gpsimd", lambda e: e.memset(consts[:, 1:2], EPS), inc=sG)
        P.do("gpsimd", lambda e: e.memset(consts[:, 4:5], EPS * (1.0 + EPS)),
             inc=sG)
        I32 = mybir.dt.int32
        P.do("gpsimd", lambda e: e.memset(
            consts[:, 5:9].bitcast(I32), 0x5F3759DF), inc=sG)
        P.do("gpsimd", lambda e: e.memset(consts[:, 2:3], 1.6e13), inc=sG)
        P.do("gpsimd", lambda e: e.memset(consts[:, 3:4], 1e12), inc=sG)
        tok_c = P.do("gpsimd", lambda e: e.memset(ones64[:], 1.0), inc=sG)

        with nc.sbuf_tensor("srows", [32, 128], F32) as srows:
            ps_sc = PSf[:, 0:32]
            t_prev_tr = None
            t_prev_cp = None
            for sd, cols, nrow in ((s1_d, s1c, 32), (s2_d, s2c, 32),
                                   (sh1_d, sh1c, 32), (sh2_d, sh2c, 32),
                                   (s3_d, s3c, 8)):
                tok_l = P.do("sync", lambda e, sd=sd, nrow=nrow: e.dma_start(
                    out=srows[0:nrow, :], in_=sd.rearrange("(a b) -> a b", b=128)),
                    waits=(t_prev_tr,), inc=dC, dma=True)
                t_tr = P.do("tensor", lambda e, nrow=nrow: e.transpose(
                    ps_sc[:, 0:nrow], srows[0:nrow, :], ident_f[0:nrow, 0:nrow]),
                    waits=(tok_l, tok_idf, t_prev_cp), inc=sT)
                t_cp = P.do("vector", lambda e, cols=cols, nrow=nrow: e.tensor_copy(
                    cols[:, 0:nrow], ps_sc[:, 0:nrow]), waits=(t_tr,), inc=sV)
                t_prev_tr = t_tr
                t_prev_cp = t_cp
            tok_scales = t_cp

        P.barrier(bar)

        def ln(src, dst, gain_b, tmp, sq_buf, stt, wts, sqs=None, sqb_c=None,
               s_legs=False):
            nm = stt[:, 0:1]
            vr = stt[:, 1:2]
            yy = stt[:, 2:3]
            tt_ = stt[:, 3:4]
            t_r = P.do("vector", lambda e: e.reduce_sum(nm, src, AX.X), waits=wts,
                       inc=sV)
            t_nm = P.do("vector", lambda e: e.tensor_scalar_mul(nm, nm, -1.0 / D),
                        waits=(t_r,), inc=sV)
            if s_legs:
                t_sq = P.do("scalar", lambda e: e.activation(
                    sq_buf, src, AF.Square, bias=nm), waits=(t_nm,), inc=sS)
                t_s2 = P.do("vector", lambda e: e.reduce_sum(vr, sq_buf, AX.X),
                            waits=(t_sq,), inc=sV)
                t_sd = P.do("scalar", lambda e: e.activation(
                    yy, vr, AF.Sqrt, bias=eps_t if sqb_c is None else sqb_c,
                    scale=(1.0 / D) if sqs is None else sqs),
                    waits=(t_s2,), inc=sS)
                t_rs = P.do("vector", lambda e: e.reciprocal(tt_, yy),
                            waits=(t_sd,), inc=sV)
                if gain_b is None:
                    return P.do("vector", lambda e: e.tensor_scalar(
                        dst, src, nm, tt_, ALU.add, ALU.mult),
                        waits=(t_rs,), inc=sV)
                t_ap = P.do("vector", lambda e: e.tensor_scalar(
                    tmp, src, nm, tt_, ALU.add, ALU.mult), waits=(t_rs,), inc=sV)
                return P.do("vector", lambda e: e.tensor_tensor(
                    dst, tmp, gain_b, ALU.mult), waits=(t_ap,), inc=sV)
            t_c = P.do("vector", lambda e: e.tensor_scalar_add(sq_buf, src, nm),
                       waits=(t_nm,), inc=sV)
            sq_dst = tmp if gain_b is not None else src
            t_q = P.do("vector", lambda e: e.tensor_mul(sq_dst, sq_buf, sq_buf),
                       waits=(t_c,), inc=sV)
            t_s2 = P.do("vector", lambda e: e.reduce_sum(vr, sq_dst, AX.X),
                        waits=(t_q,), inc=sV)
            t_ve = P.do("vector", lambda e: e.tensor_scalar(
                vr, vr, (1.0 / D) if sqs is None else sqs,
                eps_t if sqb_c is None else sqb_c, ALU.mult, ALU.add),
                waits=(t_s2,), inc=sV)
            t_rs = rsqrt_v(vr, yy, tt_, (t_ve,))
            if gain_b is None:
                return P.do("vector", lambda e: e.tensor_scalar_mul(
                    dst, sq_buf, yy), waits=(t_rs,), inc=sV)
            return P.do("vector", lambda e: e.scalar_tensor_tensor(
                dst, sq_buf, yy, gain_b, ALU.mult, ALU.mult),
                waits=(t_rs,), inc=sV)

        def rsqrt_v(v, y, t, wts):
            I32 = mybir.dt.int32
            tk = P.do("vector", lambda e: e.tensor_single_scalar(
                y.bitcast(I32), v.bitcast(I32), 1, ALU.arith_shift_right),
                waits=wts, inc=sV)
            kk = y.shape[1]
            tk = P.do("vector", lambda e, kk=kk: e.tensor_sub(
                y.bitcast(I32), magic4[:, 0:kk].bitcast(I32),
                y.bitcast(I32)), waits=(tk,), inc=sV)
            for _ in range(2):
                if kk == 1:
                    tk = P.do("vector", lambda e: e.scalar_tensor_tensor(
                        t, y, v, y, ALU.mult, ALU.mult), waits=(tk,), inc=sV)
                else:
                    tk = P.do("vector", lambda e: e.tensor_mul(t, y, y),
                              waits=(tk,), inc=sV)
                    tk = P.do("vector", lambda e: e.tensor_mul(t, t, v),
                              waits=(tk,), inc=sV)
                tk = P.do("vector", lambda e: e.tensor_scalar(
                    t, t, -0.5, 1.5, ALU.mult, ALU.add), waits=(tk,), inc=sV)
                tk = P.do("vector", lambda e: e.tensor_mul(y, y, t),
                          waits=(tk,), inc=sV)
            return tk

        fp0_toks = [None] * 8
        xb0_free = [None, None]
        sc0_free = [None]
        tok_x2p0 = {}

        def mk_fprep0_pieces(t):
            p = t % 2
            r0 = t * 128
            nm = st0[:, p * 4:p * 4 + 1]
            vr = st0[:, p * 4 + 1:p * 4 + 2]
            yy = st0[:, p * 4 + 2:p * 4 + 3]
            tt_ = st0[:, p * 4 + 3:p * 4 + 4]
            st = {}

            def pa():
                st["l"] = P.do("sync", lambda e, p=p, r0=r0: e.dma_start(
                    out=xb0[p][:], in_=x2s_d[r0:r0 + 128, :]),
                    waits=tok_x2p0[2 if t < 4 else 4] + (xb0_free[p],),
                    inc=dF[p], dma=True)
                t_r = P.do("vector", lambda e, p=p, nm=nm: e.reduce_sum(
                    nm, xb0[p][:], AX.X), waits=(st["l"],), inc=sV)
                st["nm"] = P.do("vector", lambda e, nm=nm: e.tensor_scalar_mul(
                    nm, nm, -1.0 / D), waits=(t_r,), inc=sV)

            def pb():
                t_c = P.do("vector", lambda e, p=p, nm=nm: e.tensor_scalar_add(
                    xb0[p][:], xb0[p][:], nm), waits=(st["nm"],), inc=sV)
                st["q"] = P.do("vector", lambda e, p=p: e.tensor_mul(
                    sqb[0][:], xb0[p][:], xb0[p][:]),
                    waits=(t_c, sc0_free[0]), inc=sV)

            def pc():
                t_s = P.do("vector", lambda e, vr=vr: e.reduce_sum(
                    vr, sqb[0][:], AX.X), waits=(st["q"],), inc=sV)
                sc0_free[0] = t_s
                t_ve = P.do("vector", lambda e, vr=vr: e.tensor_scalar(
                    vr, vr, 1.0 / D, eps_t, ALU.mult, ALU.add),
                    waits=(t_s,), inc=sV)
                st["rs"] = rsqrt_v(vr, yy, tt_, (t_ve,))

            def pd():
                t_f = P.do("vector", lambda e, p=p, t=t, yy=yy:
                           e.tensor_scalar_mul(
                    fbf0[:, t * D:(t + 1) * D], xb0[p][:], yy),
                    waits=(st["rs"],), inc=sV)
                xb0_free[p] = t_f
                fp0_toks[t] = t_f
            return [pa, pb, pc, pd]

        with ExitStack() as ascope:
            asb = lambda name, shape, dt: ascope.enter_context(nc.sbuf_tensor(name, shape, dt))

            gq_b = asb("gq_b", [128, D], F32)
            go_b = asb("go_b", [128, D], F32)
            wq_bf = asb("wq_bf", [128, 8 * INNER], BF16)
            wkv_bf = asb("wkv_bf", [128, 8 * 128], BF16)
            wout_bf = asb("wout_bf", [128, 4 * D], BF16)
            q_fma = asb("q_fma", [128, 4 * N], BF16)
            q_fm = [q_fma[:, c * N:(c + 1) * N] for c in range(4)]
            k_fm2 = asb("k_fm2", [128, N], BF16)
            v_ones = asb("v_ones", [128, 32 * 65], BF16)
            xt = [asb(f"xt{i}", [128, D], F32) for i in range(2)]
            sqb = [asb(f"sqb{i}", [128, D], F32) for i in range(2)]
            stats = asb("stats", [128, 64], F32)
            xn_bf = [asb(f"xn_bf{i}", [128, D], BF16) for i in range(2)]
            xn_fmt = [asb(f"xn_fmt{i}", [128, D], BF16) for i in range(2)]
            q4_bf = [asb(f"q4_bf{i}", [128, INNER], BF16) for i in range(2)]
            sq_q = asb("sq_q", [128, INNER], F32)
            sq_k = [asb(f"sq_k{i}", [128, DH], F32) for i in range(2)]
            k4_bf = [asb(f"k4_bf{i}", [128, DH], BF16) for i in range(2)]
            pT = [asb(f"pT{i}", [128, 1024], BF16) for i in range(4)]
            attn_ic = [asb(f"attn_ic{i}", [128, 4 * 512], BF16) for i in range(2)]
            oa_sb = [asb(f"oa_sb{i}", [65, 512], F32) for i in range(2)]
            ob_sb = [asb(f"ob_sb{i}", [65, 512], F32) for i in range(2)]
            rrow2 = asb("rrow2", [33, 512], F32)
            rrow2b = asb("rrow2b", [33, 512], BF16)
            bc_sb = asb("bc_sb", [64, 512], F32)
            ot_all = asb("ot_all", [128, 4 * D], F32)
            cent = asb("cent", [128, 4 * D], F32)
            t3_all = asb("t3_all", [128, 4 * D], F32)
            xr_f = wq_bf.ap().bitcast(F32)
            xr = [xr_f[:, 0:D], xr_f[:, D:2 * D]]
            x2t = xt

            P.do("sync", lambda e: e.dma_start(
                out=gq_b[:], in_=gq_d.rearrange("(o d) -> o d", o=1).broadcast_to([128, D])),
                inc=dC, dma=True)
            tok_gain = P.do("sync", lambda e: e.dma_start(
                out=go_b[:], in_=go_d.rearrange("(o d) -> o d", o=1).broadcast_to([128, D])),
                inc=dC, dma=True)
            tok_gq = tok_go = tok_gain
            P.do("gpsimd", lambda e: e.dma_start(
                out=wq_bf[:].rearrange("p (c q) -> p c q", c=8),
                in_=wq_d.rearrange("(c p) q -> p c q", p=128)),
                inc=dWA, dma=True)
            P.do("gpsimd", lambda e: e.dma_start(
                out=wkv_bf[:].rearrange("p (c q) -> p c q", c=8),
                in_=wkv_d.rearrange("(c p) q -> p c q", p=128)),
                inc=dWA, dma=True)
            tok_wall = P.do("gpsimd", lambda e: e.dma_start(
                out=wout_bf[:].rearrange("p (c q) -> p c q", c=4),
                in_=wout_d.rearrange("(c p) q -> p c q", p=128)),
                inc=dWA, dma=True)
            tok_wq = tok_wkv = tok_wout = tok_wall
            P.do("vector", lambda e: e.memset(rrow2[:], 1.0))
            tok_vo = None
            for j in range(32):
                tok_vo = P.do("vector", lambda e, j=j: e.memset(
                    v_ones[:, j * 65 + 64:j * 65 + 65], 1.0), inc=sV)

            if True:
                q_ps = [PSf[:, 0:512], PSf[:, 5 * 512:6 * 512]]
                kv_ps = [PSf[:, 512:640], PSf[:, 6 * 512:6 * 512 + 128]]
                tr_ps = [PSb[:, 2 * 1024:3 * 1024],
                         PSb[:, 7 * 1024:8 * 1024]]
                trq_ps = PSb[:, 3 * 1024:3 * 1024 + 512]
                trk_ps = PSb[0:64, 4 * 1024:4 * 1024 + 128]

                xt_free = [None, None]
                trps_free = [None, None]
                trqps_free = None
                trkps_free = None

                xnbf_free = [None, None]
                xnfmt_free = [None, None]
                q4_free = [None, None]
                k4_free = [None, None]
                qps_free = [tok_scales, tok_scales]
                kvps_free = [tok_scales, None]
                sqq_free = [None, None]
                sqk_free = [None, None]

                A_ttr = [None, None]
                Lst = [{}, {}]

                def lnA1(tc):
                    p = tc % 2
                    r0 = tc * 128
                    nm = stats[:, p * 32:p * 32 + 1]
                    tok_x = P.do("sync", lambda e, p=p, r0=r0: e.dma_start(
                        out=xt[p][:], in_=x_d[r0:r0 + 128, :]),
                        waits=(xt_free[p],), inc=dX[p], dma=True)
                    t_r = P.do("vector", lambda e, nm=nm: e.reduce_sum(
                        nm, xt[p][:], AX.X),
                        waits=(tok_x, tok_gq, xnbf_free[p]), inc=sV)
                    t_nm = P.do("vector", lambda e, nm=nm: e.tensor_scalar_mul(
                        nm, nm, -1.0 / D), waits=(t_r,), inc=sV)
                    Lst[p]["sq"] = P.do("scalar", lambda e, p=p, nm=nm:
                                        e.activation(
                        sqb[p][:], xt[p][:], AF.Square, bias=nm),
                        waits=(t_nm,), inc=sS)

                def lnA2(tc):
                    p = tc % 2
                    vr = stats[:, p * 32 + 1:p * 32 + 2]
                    sd = stats[:, p * 32 + 2:p * 32 + 3]
                    t_s2 = P.do("vector", lambda e, p=p, vr=vr: e.reduce_sum(
                        vr, sqb[p][:], AX.X), waits=(Lst[p]["sq"],), inc=sV)
                    Lst[p]["sd"] = P.do("scalar", lambda e, vr=vr, sd=sd:
                                        e.activation(
                        sd, vr, AF.Sqrt, bias=eps2_t, scale=(1.0 + EPS) / D),
                        waits=(t_s2,), inc=sS)

                def lnA3(tc):
                    p = tc % 2
                    nm = stats[:, p * 32:p * 32 + 1]
                    sd = stats[:, p * 32 + 2:p * 32 + 3]
                    rstd = stats[:, p * 32 + 3:p * 32 + 4]
                    t_rs = P.do("vector", lambda e, sd=sd, rstd=rstd:
                                e.reciprocal(rstd, sd),
                                waits=(Lst[p]["sd"],), inc=sV)
                    t_ap = P.do("vector", lambda e, p=p, nm=nm, rstd=rstd:
                                e.tensor_scalar(
                        sqb[p][:], xt[p][:], nm, rstd, ALU.add, ALU.mult),
                        waits=(t_rs,), inc=sV)
                    tok_xn = P.do("vector", lambda e, p=p: e.tensor_tensor(
                        xn_bf[p][:], sqb[p][:], gq_b[:], ALU.mult),
                        waits=(t_ap,), inc=sV)
                    xt_free[p] = tok_xn
                    ttr = None
                    for dc in range(8):
                        ttr = P.do("tensor", lambda e, dc=dc, p=p: e.transpose(
                            tr_ps[p][:, dc * 128:(dc + 1) * 128],
                            xn_bf[p][:, dc * 128:(dc + 1) * 128], ident_bf[:]),
                            waits=(tok_xn, trps_free[p]) if dc == 0 else (),
                            inc=sT if dc == 7 else None)
                    xnbf_free[p] = ttr
                    A_ttr[p] = ttr

                Bst = [{}, {}]

                def stageB_proj(tc):
                    p = tc % 2
                    tok_fmt = P.do("scalar", lambda e, p=p: e.activation(
                        xn_fmt[p][:], tr_ps[p][:], AF.Copy),
                        waits=(A_ttr[p], xnfmt_free[p]), inc=sS)
                    trps_free[p] = tok_fmt
                    tok_qmm = None
                    for dc in range(8):
                        tok_qmm = P.do("tensor", lambda e, dc=dc, p=p: e.matmul(
                            q_ps[p][:], xn_fmt[p][:, dc * 128:(dc + 1) * 128],
                            wq_bf[:, dc * INNER:(dc + 1) * INNER],
                            start=(dc == 0), stop=(dc == 7), skip_group_check=True),
                            waits=(tok_fmt, tok_wq, qps_free[p]) if dc == 0 else (),
                            inc=sT if dc == 7 else None)
                    tok_kvmm = None
                    for dc in range(8):
                        tok_kvmm = P.do("tensor", lambda e, dc=dc, p=p: e.matmul(
                            kv_ps[p][:], xn_fmt[p][:, dc * 128:(dc + 1) * 128],
                            wkv_bf[:, dc * 128:(dc + 1) * 128],
                            start=(dc == 0), stop=(dc == 7), skip_group_check=True),
                            waits=(tok_wkv, kvps_free[p]) if dc == 0 else (),
                            inc=sT if dc == 7 else None)
                    xnfmt_free[p] = tok_kvmm
                    Bst[p]["qmm"] = tok_qmm
                    Bst[p]["kvmm"] = tok_kvmm

                def epiS(tc):
                    p = tc % 2
                    Bst[p]["sqq"] = P.do("scalar", lambda e, p=p: e.activation(
                        sq_q[:], q_ps[p][:], AF.Square),
                        waits=(Bst[p]["qmm"], sqq_free[0]), inc=sS)
                    Bst[p]["sqk"] = P.do("scalar", lambda e, p=p: e.activation(
                        sq_k[p][:], kv_ps[p][:, 0:DH], AF.Square),
                        waits=(Bst[p]["kvmm"], sqk_free[p]), inc=sS)

                def epiV1(tc):
                    p = tc % 2
                    st_q = stats[:, p * 32 + 8:p * 32 + 16]
                    st_k1 = stats[:, p * 32 + 16:p * 32 + 17]
                    t_ssq = P.do("vector", lambda e, st_q=st_q: e.reduce_sum(
                        st_q, sq_q[:].rearrange("p (h d) -> p h d", h=8),
                        AX.X), waits=(Bst[p]["sqq"],), inc=sV)
                    sqq_free[0] = t_ssq
                    t_ssk = P.do("vector", lambda e, p=p, st_k1=st_k1: e.reduce_sum(
                        st_k1, sq_k[p][:], AX.X), waits=(Bst[p]["sqk"],), inc=sV)
                    sqk_free[p] = t_ssk
                    Bst[p]["ssq"] = t_ssq
                    Bst[p]["ssk"] = t_ssk

                def epiR(tc):
                    nonlocal trqps_free, trkps_free
                    p = tc % 2
                    r0 = tc * 128
                    st_q = stats[:, p * 32 + 8:p * 32 + 16]
                    st_k1 = stats[:, p * 32 + 16:p * 32 + 17]
                    st_qk = stats[:, p * 32 + 8:p * 32 + 17]
                    t_nqk = P.do("scalar", lambda e, st_qk=st_qk:
                                 e.activation(st_qk, st_qk, AF.Sqrt),
                                 waits=(Bst[p]["ssq"], Bst[p]["ssk"]), inc=sS)
                    t_rqk = P.do("vector", lambda e, st_qk=st_qk: e.reciprocal(
                        st_qk, st_qk), waits=(t_nqk,), inc=sV)
                    t_fac = P.do("vector", lambda e, st_q=st_q: e.tensor_scalar(
                        st_q, st_q, clamp_q, 4.0,
                        ALU.min, ALU.mult), waits=(t_rqk,), inc=sV)
                    t_fk = P.do("vector", lambda e, st_k1=st_k1: e.tensor_scalar(
                        st_k1, st_k1, clamp_k, 4.0,
                        ALU.min, ALU.mult), inc=sV)
                    tq4 = P.do("vector", lambda e, p=p, st_q=st_q: e.tensor_tensor(
                        q4_bf[p][:].rearrange("p (h d) -> p h d", h=8),
                        q_ps[p][:].rearrange("p (h d) -> p h d", h=8),
                        st_q.rearrange("p (h o) -> p h o", o=1)
                            .broadcast_to([128, 8, DH]),
                        ALU.mult), waits=(t_fac, q4_free[p]), inc=sV)
                    qps_free[p] = tq4
                    ttrq = None
                    for c in range(4):
                        ttrq = P.do("tensor", lambda e, c=c, p=p: e.transpose(
                            trq_ps[:, c * 128:(c + 1) * 128],
                            q4_bf[p][:, c * 128:(c + 1) * 128], ident_bf[:]),
                            waits=(tq4, trqps_free) if c == 0 else (),
                            inc=sT if c == 3 else None)
                    q4_free[p] = ttrq
                    tcq = P.do("scalar", lambda e, r0=r0: e.activation(
                        q_fma.rearrange("p (c n) -> p c n", c=4)[:, :, r0:r0 + 128],
                        trq_ps[:].rearrange("p (c w) -> p c w", c=4), AF.Copy),
                        waits=(ttrq,), inc=sS)
                    trqps_free = tcq
                    tk4 = P.do("vector", lambda e, p=p, st_k1=st_k1:
                               e.tensor_scalar_mul(
                        k4_bf[p][:], kv_ps[p][:, 0:DH], st_k1),
                        waits=(t_fk, k4_free[p]), inc=sV)
                    t_trk = P.do("tensor", lambda e, p=p: e.transpose(
                        trk_ps[:], k4_bf[p][:], ident_bf[:]),
                        waits=(tk4, trkps_free), inc=sT)
                    k4_free[p] = t_trk
                    P.do("scalar", lambda e, r0=r0: e.activation(
                        k_fm2[0:64, r0:r0 + 128], trk_ps[:], AF.Copy),
                        waits=(t_trk,))
                    tkc2 = P.do("scalar", lambda e, r0=r0: e.activation(
                        k_fm2[64:128, r0:r0 + 128], trk_ps[:], AF.Copy), inc=sS)
                    trkps_free = tkc2
                    tvv = P.do("scalar", lambda e, tc=tc, p=p: e.activation(
                        v_ones[:, tc * 65:tc * 65 + 64], kv_ps[p][:, DH:2 * DH],
                        AF.Copy),
                        waits=(tok_vo,) if tc == 0 else (), inc=sS)
                    kvps_free[p] = tvv

                for w in (0, 1):
                    if w < NTC:
                        lnA1(w)
                        lnA2(w)
                        lnA3(w)
                stageB_proj(0)
                for tc in range(NTC):
                    if tc + 1 < NTC:
                        stageB_proj(tc + 1)
                    if tc + 2 < NTC:
                        lnA1(tc + 2)
                        lnA2(tc + 2)
                    epiS(tc)
                    epiV1(tc)
                    if tc + 2 < NTC:
                        lnA3(tc + 2)
                    epiR(tc)

                tok_p12_V = (sV, sV.n, P.gidx)
                tok_p12_T = (sT, sT.n, P.gidx)

            if "xn" in tap_d:
                last_tap[0] = P.do("gpsimd", lambda e: e.dma_start(
                    out=tap_d["xn"], in_=xn_bf[1][:]),
                    waits=(tok_p12_V, last_tap[0]), inc=dT, dma=True)
            if "qfm" in tap_d:
                last_tap[0] = P.do("gpsimd", lambda e: e.dma_start(
                    out=tap_d["qfm"], in_=q_fm[0][:]),
                    waits=(tok_p12_V, last_tap[0]), inc=dT, dma=True)
            if "kfm" in tap_d:
                last_tap[0] = P.do("gpsimd", lambda e: e.dma_start(
                    out=tap_d["kfm"], in_=k_fm2[:]),
                    waits=(tok_p12_V, last_tap[0]), inc=dT, dma=True)
            if "vones" in tap_d:
                last_tap[0] = P.do("gpsimd", lambda e: e.dma_start(
                    out=tap_d["vones"], in_=v_ones[:]),
                    waits=(tok_p12_V, last_tap[0]), inc=dT, dma=True)

            P.barrier(bar)

            if True:
                S_AB = [PSf[:, 0:1024], PSf[:, 1024:2048]]
                O_A = PSf[0:65, 2048:2560]
                O_B = PSf[0:65, 2560:3072]
                bc_ps = PSf[0:64, 3072:3584]
                op_ps = PSf[:, 3584:4096]

                sab_free = [None, None]
                pT_free = [None, None, None, None]
                o_free = None
                bc_free = None
                rrow_free = None
                oa_free = [None, None]
                attn_free = [None, None]
                opps_free = None
                otall_free = None
                x2t_free = [None, None]
                xr_free = [None, None]
                pending_norm = [[]]
                norm_last = [None]
                bcsb_free = [None]
                sched = {}

                def sched_add(gp, jt, fn):
                    sched.setdefault((gp, jt), []).append(fn)

                def emit_norm_recip(ic, pair, tva_, tvb_):
                    nonlocal rrow_free
                    pp2 = pair % 2
                    cpa = P.do("vector", lambda e, pp2=pp2: e.tensor_copy(
                        rrow2[0:1, :], oa_sb[pp2][64:65, :]),
                        waits=(tva_, rrow_free), inc=sV)
                    cpb = P.do("vector", lambda e, pp2=pp2: e.tensor_copy(
                        rrow2[32:33, :], ob_sb[pp2][64:65, :]),
                        waits=(tvb_,), inc=sV)
                    t_rc = P.do("vector", lambda e: e.reciprocal(
                        rrow2[:], rrow2[:]), waits=(cpb,), inc=sV)
                    t_cb = P.do("vector", lambda e: e.tensor_copy(
                        rrow2b[:], rrow2[:]), waits=(t_rc,), inc=sV)
                    return t_cb

                def make_norm_bc(ic, pair, t_rc, half):
                    pp2 = pair % 2
                    aic = ic % 2

                    def emit():
                        nonlocal bc_free, rrow_free
                        row0 = 0 if half == 0 else 32
                        o_sb = oa_sb if half == 0 else ob_sb
                        prow = 0 if half == 0 else 64
                        tb = P.do("tensor", lambda e, row0=row0: e.matmul(
                            bc_ps[:], ones64[row0:row0 + 1, :],
                            rrow2b[row0:row0 + 1, :],
                            start=True, stop=True, skip_group_check=True),
                            waits=(t_rc, bc_free, tok_c), inc=sT)
                        if half == 1:
                            rrow_free = tb
                        tc_ = P.do("vector", lambda e: e.tensor_copy(
                            bc_sb[:], bc_ps[:]), waits=(tb, bcsb_free[0]), inc=sV)
                        bc_free = tc_
                        tt = P.do("vector", lambda e, pp2=pp2, aic=aic, pair=pair,
                                  prow=prow, o_sb=o_sb: e.tensor_tensor(
                            attn_ic[aic][prow:prow + 64,
                                         pair * 512:(pair + 1) * 512],
                            o_sb[pp2][0:64, :], bc_sb[:], ALU.mult),
                            waits=(tc_, attn_free[aic] if half == 0 else None),
                            inc=sV)
                        bcsb_free[0] = tt
                        if half == 1:
                            oa_free[pp2] = tt
                        return tt
                    return emit

                norm_tok = {}

                def emit_phase4(ic, flush=False):
                    aic = ic % 2
                    i0 = ic * 512
                    m3 = stats[:, 20:24]
                    r3 = stats[:, 24:28]
                    m4 = stats[:, 28:32]
                    r4 = stats[:, 52:56]
                    tok_xr = [None] * 4
                    op2_free = [None]
                    state = {}

                    def mk_group(tcc, nh):
                        def emit():
                            nonlocal opps_free, otall_free
                            if tcc == 0 and nh == 0:
                                for t2 in range(2):
                                    r0 = i0 + t2 * 128
                                    tok_xr[t2] = P.do("sync",
                                        lambda e, pp=t2, r0=r0: e.dma_start(
                                            out=xr[pp][:],
                                            in_=x_d[r0:r0 + 128, :]),
                                        waits=(xr_free[t2],
                                               tok_p12_T if ic == 0 else None),
                                        inc=dX[t2], dma=True)
                            gi = tcc * 2 + nh
                            if flush and gi % 2 == 1:
                                ps = PSf[:, 0:512]
                                ps_free = op2_free[0]
                            else:
                                ps = op_ps
                                ps_free = opps_free
                            tmm = None
                            for c in range(4):
                                w0 = ()
                                if c == 0:
                                    w0 = (norm_last[0], tok_wout, ps_free,
                                          otall_free if (tcc == 0 and nh == 0)
                                          else None)
                                tmm = P.do("tensor", lambda e, c=c, nh=nh,
                                           tcc=tcc, aic=aic, ps=ps: e.matmul(
                                    ps[:],
                                    attn_ic[aic][:, c * 512 + tcc * 128:
                                                 c * 512 + tcc * 128 + 128],
                                    wout_bf[:, c * D + nh * 512:
                                            c * D + nh * 512 + 512],
                                    start=(c == 0), stop=(c == 3),
                                    skip_group_check=True),
                                    waits=w0, inc=sT if c == 3 else None)
                                if tcc == 3 and nh == 1 and c == 3:
                                    attn_free[aic] = tmm
                            t_ot = P.do("vector", lambda e, nh=nh, tcc=tcc,
                                        ps=ps: e.tensor_copy(
                                ot_all[:, tcc * D + nh * 512:
                                       tcc * D + nh * 512 + 512], ps[:]),
                                waits=(tmm,), inc=sV)
                            if flush and gi % 2 == 1:
                                op2_free[0] = t_ot
                            else:
                                opps_free = t_ot
                            state["t_ot"] = t_ot
                        return emit

                    def ln3_a():
                        t_r3 = P.do("vector", lambda e: e.reduce_sum(
                            m3, ot_all[:].rearrange("p (t d) -> p t d", t=4),
                            AX.X), waits=(state["t_ot"],), inc=sV)
                        state["t_m3"] = P.do("vector", lambda e:
                                             e.tensor_scalar_mul(
                            m3, m3, -1.0 / D), waits=(t_r3,), inc=sV)

                    def ln3_b():
                        t_c3 = None
                        for tcc in range(4):
                            t_c3 = P.do("vector", lambda e, tcc=tcc:
                                        e.tensor_scalar_add(
                                cent[:, tcc * D:(tcc + 1) * D],
                                ot_all[:, tcc * D:(tcc + 1) * D],
                                m3[:, tcc:tcc + 1]),
                                waits=(state["t_m3"],) if tcc == 0 else (),
                                inc=sV if tcc == 3 else None)
                        state["t_c3"] = t_c3

                    def ln3_c():
                        if flush:
                            state["t_sq3"] = P.do("scalar", lambda e:
                                                  e.activation(
                                t3_all[:], cent[:], AF.Square),
                                waits=(state["t_c3"],), inc=sS)
                        else:
                            state["t_sq3"] = P.do("vector", lambda e:
                                                  e.tensor_mul(
                                t3_all[:], cent[:], cent[:]),
                                waits=(state["t_c3"],), inc=sV)

                    def ln3_d():
                        t_s23 = P.do("vector", lambda e: e.reduce_sum(
                            r3, t3_all[:].rearrange("p (t d) -> p t d", t=4),
                            AX.X), waits=(state["t_sq3"],), inc=sV)
                        state["t_ve3"] = P.do("vector", lambda e: e.tensor_scalar(
                            r3, r3, 1.0 / D, eps_t, ALU.mult, ALU.add),
                            waits=(t_s23,), inc=sV)

                    def ln3_e():
                        y3 = stats[:, 56:60]
                        state["y3"] = y3
                        state["t_rs3"] = rsqrt_v(r3, y3, stats[:, 60:64],
                                                 (state["t_ve3"],))

                    def ln3_f():
                        nonlocal otall_free
                        y3 = state["y3"]
                        t_a3 = None
                        for tcc in range(4):
                            t_a3 = P.do("vector", lambda e, tcc=tcc, y3=y3:
                                        e.scalar_tensor_tensor(
                                t3_all[:, tcc * D:(tcc + 1) * D],
                                cent[:, tcc * D:(tcc + 1) * D],
                                y3[:, tcc:tcc + 1], go_b[:], ALU.mult,
                                ALU.mult),
                                waits=(state["t_rs3"], tok_go)
                                      if tcc == 0 else (),
                                inc=sV if tcc == 3 else None)
                        state["t_a3"] = t_a3

                    def ln4_a():
                        nonlocal otall_free
                        t_r4 = P.do("vector", lambda e: e.reduce_sum(
                            m4, t3_all[:].rearrange("p (t d) -> p t d", t=4),
                            AX.X), waits=(state["t_a3"],), inc=sV)
                        t_m4 = P.do("vector", lambda e: e.tensor_scalar_mul(
                            m4, m4, -1.0 / D), waits=(t_r4,), inc=sV)
                        t_c4 = None
                        for tcc in range(4):
                            t_c4 = P.do("vector", lambda e, tcc=tcc:
                                        e.tensor_scalar_add(
                                cent[:, tcc * D:(tcc + 1) * D],
                                t3_all[:, tcc * D:(tcc + 1) * D],
                                m4[:, tcc:tcc + 1]),
                                waits=(t_m4,) if tcc == 0 else (),
                                inc=sV if tcc == 3 else None)
                        otall_free = t_c4
                        state["t_c4"] = t_c4

                    def ln4_b():
                        if flush:
                            state["t_sq4"] = P.do("scalar", lambda e:
                                                  e.activation(
                                t3_all[:], cent[:], AF.Square),
                                waits=(state["t_c4"],), inc=sS)
                        else:
                            state["t_sq4"] = P.do("vector", lambda e:
                                                  e.tensor_mul(
                                t3_all[:], cent[:], cent[:]),
                                waits=(state["t_c4"],), inc=sV)

                    def ln4_c():
                        t_s24 = P.do("vector", lambda e: e.reduce_sum(
                            r4, t3_all[:].rearrange("p (t d) -> p t d", t=4),
                            AX.X), waits=(state["t_sq4"],), inc=sV)
                        t_ve4 = P.do("vector", lambda e: e.tensor_scalar(
                            r4, r4, 1.0 / D, eps_t, ALU.mult, ALU.add),
                            waits=(t_s24,), inc=sV)
                        state["y4"] = stats[:, 56:60]
                        state["t_rs4"] = rsqrt_v(r4, state["y4"],
                                                 stats[:, 60:64], (t_ve4,))

                    def ln4_d():
                        t_rs4 = state["t_rs4"]
                        y4 = state["y4"]
                        for tcc in range(4):
                            r0 = i0 + tcc * 128
                            pp = tcc % 2
                            t_x2 = P.do("vector", lambda e, tcc=tcc, pp=pp,
                                        y4=y4: e.scalar_tensor_tensor(
                                x2t[pp][:], cent[:, tcc * D:(tcc + 1) * D],
                                y4[:, tcc:tcc + 1], xr[pp][:], ALU.mult,
                                ALU.add),
                                waits=(t_rs4 if tcc == 0 else None,
                                       tok_xr[tcc], x2t_free[pp]), inc=sV)
                            xr_free[pp] = t_x2
                            if tcc < 2:
                                r2 = i0 + (tcc + 2) * 128
                                tok_xr[tcc + 2] = P.do("sync",
                                    lambda e, pp=pp, r2=r2: e.dma_start(
                                        out=xr[pp][:],
                                        in_=x_d[r2:r2 + 128, :]),
                                    waits=(t_x2,), inc=dX[pp], dma=True)
                            t_st = P.do("sync", lambda e, r0=r0, pp=pp:
                                        e.dma_start(
                                out=x2s_d[r0:r0 + 128, :], in_=x2t[pp][:]),
                                waits=(t_x2,), inc=dS if pp == 0 else dS2,
                                dma=True)
                            x2t_free[pp] = t_st

                    closures = [mk_group(tcc, nh)
                                for tcc in range(4) for nh in range(2)]
                    if flush:
                        for fn in closures:
                            fn()
                        for fn in (ln3_a, ln3_b, ln3_c, ln3_d, ln3_e, ln3_f,
                                   ln4_a, ln4_b, ln4_c, ln4_d):
                            fn()
                    else:
                        gp0 = (ic + 1) * 4
                        slots = [(0, 21), (0, 24), (0, 27), (1, 3), (1, 7),
                                 (1, 13), (1, 21), (1, 27)]
                        for fn, (dp, jt) in zip(closures, slots):
                            sched_add(gp0 + dp, jt, fn)
                        for fn, jt in zip((ln3_a, ln3_b, ln3_c, ln3_d, ln3_e,
                                           ln3_f),
                                          (4, 7, 10, 13, 17, 22)):
                            sched_add(gp0 + 2, jt, fn)
                        for fn, jt in zip((ln4_a, ln4_b, ln4_c, ln4_d),
                                          (4, 8, 12, 17)):
                            sched_add(gp0 + 3, jt, fn)

                fprep0_on = (n_ic == 8 and n_pair == 4 and phase_limit >= 3
                             and n_pass > 0)
                fprep0_sched = {(4, 2): 0, (4, 3): 1, (5, 0): 2, (5, 1): 3,
                                (5, 2): 4, (5, 3): 5, (6, 0): 6, (6, 1): 7}
                for ic in range(n_ic if phase_limit >= 2 else 0):
                    i0 = ic * 512
                    if ic == 4 and fprep0_on:
                        tok_x2p0[2] = tok_x2p0[4] = ((dS, dS.n, P.gidx),
                                                     (dS2, dS2.n, P.gidx))
                    for pair in range(n_pair):
                        tok_exp = {}
                        tok_pv = None

                        def emit_pv(jp):
                            nonlocal tok_pv
                            pp4 = jp % 4
                            P.do("tensor", lambda e, pp4=pp4, jp=jp: e.matmul(
                                O_A[:], v_ones[:, jp * 65:(jp + 1) * 65],
                                pT[pp4][:, 0:512],
                                start=(jp == 0), stop=(jp == 31),
                                skip_group_check=True),
                                waits=(tok_exp[jp], o_free) if jp == 0 else
                                      (tok_exp[jp],))
                            tok_pv = P.do("tensor", lambda e, pp4=pp4, jp=jp:
                                          e.matmul(
                                O_B[:], v_ones[:, jp * 65:(jp + 1) * 65],
                                pT[pp4][:, 512:1024],
                                start=(jp == 0), stop=(jp == 31),
                                skip_group_check=True), inc=sT)
                            pT_free[pp4] = tok_pv

                        for jt in range(32):
                            p2 = jt % 2
                            p4 = jt % 4
                            j0 = jt * 128
                            P.do("tensor", lambda e, p2=p2, j0=j0, i0=i0, pair=pair:
                                 e.matmul(S_AB[p2][:, 0:512],
                                          k_fm2[0:64, j0:j0 + 128],
                                          q_fm[pair][0:64, i0:i0 + 512],
                                          start=True, stop=True,
                                          tile_position=(0, 0),
                                          skip_group_check=True),
                                 waits=(sab_free[p2],))
                            t_qk = P.do("tensor", lambda e, p2=p2, j0=j0, i0=i0,
                                        pair=pair:
                                        e.matmul(S_AB[p2][:, 512:1024],
                                                 k_fm2[64:128, j0:j0 + 128],
                                                 q_fm[pair][64:128, i0:i0 + 512],
                                                 start=True, stop=True,
                                                 tile_position=(64, 0),
                                                 skip_group_check=True),
                                        inc=sT)
                            t_e = P.do("scalar", lambda e, p2=p2, p4=p4: e.activation(
                                pT[p4][:], S_AB[p2][:], AF.Exp, bias=neg16),
                                waits=(t_qk, pT_free[p4]), inc=sS)
                            tok_exp[jt] = t_e
                            sab_free[p2] = t_e
                            gp = ic * 4 + pair
                            for fn in sched.pop((gp, jt), ()):
                                fn()
                            if jt == 10 and pending_norm[0]:
                                pending_norm[0].pop(0)()
                            if jt == 18 and pending_norm[0]:
                                norm_last[0] = pending_norm[0].pop(0)()
                            if jt >= 2:
                                emit_pv(jt - 2)
                        emit_pv(30)
                        emit_pv(31)
                        pp2 = pair % 2
                        tva = P.do("vector", lambda e, pp2=pp2: e.tensor_copy(
                            oa_sb[pp2][:], O_A[:]),
                            waits=(tok_pv, oa_free[pp2]), inc=sV)
                        tvb = P.do("vector", lambda e, pp2=pp2: e.tensor_copy(
                            ob_sb[pp2][:], O_B[:]), inc=sV)
                        o_free = tvb
                        t_rc = emit_norm_recip(ic, pair, tva, tvb)
                        pending_norm[0] = [make_norm_bc(ic, pair, t_rc, 0),
                                           make_norm_bc(ic, pair, t_rc, 1)]
                        if (ic, pair) in fprep0_sched and fprep0_on:
                            gpn = ic * 4 + pair + 1
                            for fn, jtp in zip(
                                    mk_fprep0_pieces(fprep0_sched[(ic, pair)]),
                                    (20, 23, 26, 29)):
                                sched_add(gpn, jtp, fn)
                    if ic == n_ic - 1 and pending_norm[0]:
                        pending_norm[0].pop(0)()
                        norm_last[0] = pending_norm[0].pop(0)()
                    if n_pair == 4:
                        emit_phase4(ic, flush=(ic == n_ic - 1))
                if phase_limit >= 2 and n_ic > 0 and n_pair == 4:
                    for key in sorted(sched):
                        for fn in sched.pop(key):
                            fn()

                tok_x2_done = (dS, dS.n, P.gidx)
                tok_x2_done2 = (dS2, dS2.n, P.gidx)

        P.barrier(bar)

        with ExitStack() as fscope:
            fsb = lambda name, shape, dt: fscope.enter_context(nc.sbuf_tensor(name, shape, dt))
            fps = lambda name, shape, dt: fscope.enter_context(nc.psum_tensor(name, shape, dt))

            f_fm = fsb("f_fm", [128, 8 * TT], BF16)
            h_fm = fsb("h_fm", [128, 32 * TT], BF16)
            u1_fm = fsb("u1_fm", [128, 32 * TT], BF16)
            w12s = [fsb(f"w12s{i}", [128, 2048], BF16) for i in range(2)]
            h1s = [fsb(f"h1s{i}", [128, 4096], BF16) for i in range(2)]
            sil = [fsb(f"sil{i}", [128, 512], F32) for i in range(2)]
            oscb = fsb("oscb", [128, 2048], BF16)
            osc = [oscb[:, i * 512:(i + 1) * 512] for i in range(4)]
            xr2 = fsb("xr2", [128, D], F32)
            xr3 = xr2
            f_bf = fbf0
            ffn_tm = f_bf
            outt = xb0[0]
            sq5 = xb0[1]
            stats5 = fsb("stats5", [128, 8], F32)

            psG1 = [PSf[:, 0:512], PSf[:, 512:1024]]
            psG2 = [PSf[:, 1024:1536], PSf[:, 1536:2048]]
            psU = [PSf[:, 2048:2560], PSf[:, 2560:3072]]
            trf_ps = PSb[:, 6 * 1024:7 * 1024]
            trf_ps2 = PSb[:, 7 * 1024:8 * 1024]
            psT_par = [PSb[:, 7 * 1024:7 * 1024 + 128],
                       PSb[:, 6 * 1024:6 * 1024 + 128],
                       PSb[:, 3 * 1024:3 * 1024 + 128],
                       PSb[:, 2 * 1024:2 * 1024 + 128]]

            w1s_free = [None, None]
            w2s_free = [None, None]
            psG1_free = [None, None]
            h1s_free = [None, None]
            psG_free = [None, None]
            psU_free = [None, None]
            sil_free = [None, None]
            osc_free = [None, None]
            trf_free = [None, None]
            pst_free = [None, None, None, None]
            xr2_free = None
            xr3_free = None
            sq5_free = None
            fbf_free = [None] * 8
            t_hfm_lastread = None
            t_ffm_lastread = None
            t_u1_lastread = None
            x2_ready = tok_x2_done
            x2_ready2 = tok_x2_done2
            outt_free = None
            ffntm_free = None
            tap_tok_hfm = None
            tap_tok_u1fm = None
            tap_tok_ffntm = None

            fp_toks = []
            fprep_v_half = [None, None]
            fprep_state = {"ffntm_free": None}
            ln6_q = []

            def emit_fprep_chain(pz, tcc):
                nonlocal xr2_free, sq5_free, outt_free
                if tcc == 0:
                    fp_toks.clear()
                t0 = pz * TT
                nm = stats5[:, 0:1]
                vr = stats5[:, 1:2]
                yy = stats5[:, 2:3]
                tt_ = stats5[:, 3:4]
                if True:
                    r0 = t0 + tcc * 128
                    if tcc % 2 == 0:
                        src = xr2
                        w_src = (xr2_free, xr3_free)
                        sem = dX[0]
                    else:
                        src = sq5
                        w_src = (sq5_free,)
                        sem = dW2[0]
                    dst = f_bf[:, tcc * D:(tcc + 1) * D]
                    tok_xl = P.do("sync", lambda e, r0=r0, src=src: e.dma_start(
                        out=src[:], in_=x2s_d[r0:r0 + 128, :]),
                        waits=(x2_ready, x2_ready2) + w_src, inc=sem, dma=True)
                    t_r = P.do("vector", lambda e, src=src: e.reduce_sum(
                        nm, src[:], AX.X), waits=(tok_xl,), inc=sV)
                    t_nm = P.do("vector", lambda e: e.tensor_scalar_mul(
                        nm, nm, -1.0 / D), waits=(t_r,), inc=sV)
                    t_c = P.do("vector", lambda e, src=src: e.tensor_scalar_add(
                        src[:], src[:], nm), waits=(t_nm,), inc=sV)
                    t_q = P.do("vector", lambda e, src=src: e.tensor_mul(
                        outt[:], src[:], src[:]), waits=(t_c, outt_free),
                        inc=sV)
                    t_s = P.do("vector", lambda e: e.reduce_sum(
                        vr, outt[:], AX.X), waits=(t_q,), inc=sV)
                    outt_free = t_s
                    t_ve = P.do("vector", lambda e: e.tensor_scalar(
                        vr, vr, 1.0 / D, eps_t, ALU.mult, ALU.add),
                        waits=(t_s,), inc=sV)
                    t_rs = rsqrt_v(vr, yy, tt_, (t_ve,))
                    t_f = P.do("vector", lambda e, src=src, dst=dst:
                               e.tensor_scalar_mul(dst, src[:], yy),
                        waits=(t_rs, fbf_free[tcc], last_tap[0]), inc=sV)
                    if tcc % 2 == 0:
                        xr2_free = t_f
                    else:
                        sq5_free = t_f
                    fp_toks.append(t_f)

            def emit_fprep_ln(pz):
                for tcc in range(8):
                    emit_fprep_chain(pz, tcc)

            def emit_fprep_ln0():
                nonlocal xr2_free, sq5_free, outt_free
                fp_toks.clear()
                C = {}

                def sl(t):
                    b = 4 * (t % 2)
                    return (stats5[:, b:b + 1], stats5[:, b + 1:b + 2],
                            stats5[:, b + 2:b + 3], stats5[:, b + 3:b + 4])

                def srcb(t):
                    return xr2 if t % 2 == 0 else sq5

                def c_load(t):
                    nonlocal xr2_free, sq5_free
                    r0 = t * 128
                    w_src = (xr2_free,) if t % 2 == 0 else (sq5_free,)
                    sem = dX[0] if t % 2 == 0 else dW2[0]
                    C[t, "xl"] = P.do("sync", lambda e, r0=r0, t=t: e.dma_start(
                        out=srcb(t)[:], in_=x2s_d[r0:r0 + 128, :]),
                        waits=(x2_ready, x2_ready2) + w_src, inc=sem, dma=True)

                def c_red(t):
                    nm = sl(t)[0]
                    t_r = P.do("vector", lambda e, nm=nm, t=t: e.reduce_sum(
                        nm, srcb(t)[:], AX.X), waits=(C[t, "xl"],), inc=sV)
                    C[t, "nm"] = P.do("vector", lambda e, nm=nm:
                                      e.tensor_scalar_mul(nm, nm, -1.0 / D),
                                      waits=(t_r,), inc=sV)

                def c_sq(t):
                    nm = sl(t)[0]
                    C[t, "sq"] = P.do("scalar", lambda e, nm=nm, t=t:
                                      e.activation(
                        outt[:], srcb(t)[:], AF.Square, bias=nm),
                        waits=(C[t, "nm"], C.get((t - 1, "s2")), outt_free),
                        inc=sS)

                def c_mid(t):
                    nm, vr, sd, rstd = sl(t)
                    t_s = P.do("vector", lambda e, vr=vr: e.reduce_sum(
                        vr, outt[:], AX.X), waits=(C[t, "sq"],), inc=sV)
                    C[t, "s2"] = t_s
                    C[t, "sd"] = P.do("scalar", lambda e, vr=vr, sd=sd:
                                      e.activation(
                        sd, vr, AF.Sqrt, bias=eps_t, scale=1.0 / D),
                        waits=(t_s,), inc=sS)

                def c_ap(t):
                    nonlocal xr2_free, sq5_free, outt_free
                    nm, vr, sd, rstd = sl(t)
                    dst = f_bf[:, t * D:(t + 1) * D]
                    t_rs = P.do("vector", lambda e, sd=sd, rstd=rstd:
                                e.reciprocal(rstd, sd),
                                waits=(C[t, "sd"],), inc=sV)
                    t_f = P.do("vector", lambda e, nm=nm, rstd=rstd, dst=dst,
                               t=t: e.tensor_scalar(
                        dst, srcb(t)[:], nm, rstd, ALU.add, ALU.mult),
                        waits=(t_rs, fbf_free[t], last_tap[0]), inc=sV)
                    if t % 2 == 0:
                        xr2_free = t_f
                    else:
                        sq5_free = t_f
                    outt_free = C[t, "s2"]
                    fp_toks.append(t_f)

                c_load(0)
                c_red(0)
                c_sq(0)
                for t in range(8):
                    c_mid(t)
                    if t + 1 < 8:
                        c_load(t + 1)
                        c_red(t + 1)
                    c_ap(t)
                    if t + 1 < 8:
                        c_sq(t + 1)

            def emit_fprep_tr(pz, src_bf=None, toks=None):
                nonlocal tok_fprep_v
                sbf = f_bf if src_bf is None else src_bf
                tks = fp_toks if toks is None else toks
                for tcc in range(8):
                    tb = tcc % 2
                    tps = trf_ps if tb == 0 else trf_ps2
                    ttr = None
                    for dc in range(8):
                        ttr = P.do("tensor", lambda e, dc=dc, tcc=tcc, tps=tps,
                                   sbf=sbf: e.transpose(
                            tps[:, dc * 128:(dc + 1) * 128],
                            sbf[:, tcc * D + dc * 128:tcc * D + dc * 128 + 128],
                            ident_bf[:]),
                            waits=(tks[tcc], trf_free[tb], t_ffm_lastread,
                                   pst_free[1 - tb]) if dc == 0 else (),
                            inc=sT if dc == 7 else None)
                    fbf_free[tcc] = ttr
                    tcp = P.do("scalar", lambda e, tcc=tcc, tps=tps:
                               e.activation(
                        f_fm.rearrange("p (c t) -> p c t", t=TT)
                            [:, :, tcc * 128:(tcc + 1) * 128],
                        tps[:].rearrange("p (c w) -> p c w", w=128),
                        AF.Copy), waits=(ttr,), inc=sS)
                    trf_free[tb] = tcp
                    tok_fprep_v = tcp
                    if tcc == 3:
                        fprep_v_half[0] = tcp
                    elif tcc == 7:
                        fprep_v_half[1] = tcp
                pst_free[1] = trf_free[0]
                pst_free[0] = trf_free[1]
                fprep_state["ffntm_free"] = (sT, sT.n, P.gidx)

            tok_fprep_v = None
            if phase_limit >= 3 and n_pass > 0:
                if fp0_toks[7] is not None:
                    emit_fprep_tr(0, src_bf=fbf0, toks=fp0_toks)
                else:
                    emit_fprep_ln0()
                    emit_fprep_tr(0)

            for pz in range(n_pass if phase_limit >= 3 else 0):
                t0 = pz * TT
                t_hlast = None
                t_glast_mm = None
                h1_pre = {}
                for jc in range(32 if ffn_stage >= 2 else 0):
                    if ln6_q and jc % 2 == 0:
                        ln6_q.pop(0)()
                    if jc in (26, 29) and ffn_stage >= 3:
                        oc0 = 0 if jc == 26 else 1
                        h1_pre[oc0] = P.do("gpsimd", lambda e, oc0=oc0:
                                           e.dma_start(
                            out=h1s[oc0][:],
                            in_=h1_d[oc0].rearrange("p a b -> p (a b)")),
                            waits=(h1s_free[oc0],), inc=dWH[oc0], dma=True)
                    pj = jc % 2
                    tok_w1 = tok_w2 = P.do("gpsimd", lambda e, jc=jc, pj=pj:
                                           e.dma_start(
                        out=w12s[pj][:],
                        in_=w12_d[jc].rearrange("p a b -> p (a b)")),
                        waits=(w1s_free[pj],), inc=dW1[pj], dma=True)
                    for nch in range(2):
                        pb = nch
                        pst_h = pst_free[3 - pb] if jc == 0 else None
                        tm1 = None
                        for dc in range(8):
                            w0 = ()
                            if dc == 0:
                                w0 = (tok_w1, psG_free[pb], psG1_free[pb],
                                      fprep_v_half[nch], tok_scales,
                                      t_hfm_lastread)
                            tm1 = P.do("tensor", lambda e, dc=dc, pj=pj, nch=nch, pb=pb:
                                       e.matmul(
                                psG1[pb][:], w12s[pj][:, dc * 128:(dc + 1) * 128],
                                f_fm[:, dc * TT + nch * 512:dc * TT + nch * 512 + 512],
                                start=(dc == 0), stop=(dc == 7),
                                skip_group_check=True),
                                waits=w0, inc=sT if dc == 7 else None)
                        tm2 = None
                        for dc in range(8):
                            w0 = (tok_w2, pst_h) if dc == 0 else ()
                            tm2 = P.do("tensor", lambda e, dc=dc, pj=pj, nch=nch, pb=pb:
                                       e.matmul(
                                psG2[pb][:],
                                w12s[pj][:, 1024 + dc * 128:1024 + (dc + 1) * 128],
                                f_fm[:, dc * TT + nch * 512:dc * TT + nch * 512 + 512],
                                start=(dc == 0), stop=(dc == 7),
                                skip_group_check=True),
                                waits=w0, inc=sT if dc == 7 else None)
                        if nch == 1:
                            w1s_free[pj] = tm2
                            t_glast_mm = tm2
                        t_sil = P.do("scalar", lambda e, pb=pb, jc=jc: e.activation(
                            sil[pb][:], psG1[pb][:], AF.Silu,
                            scale=s1c[:, jc:jc + 1]),
                            waits=(tm1, sil_free[pb]), inc=sS)
                        psG1_free[pb] = t_sil
                        t_h = P.do("vector", lambda e, pb=pb, jc=jc, nch=nch:
                                   e.scalar_tensor_tensor(
                            h_fm[:, jc * TT + nch * 512:jc * TT + nch * 512 + 512],
                            psG2[pb][:], s2c[:, jc:jc + 1], sil[pb][:],
                            ALU.mult, ALU.mult),
                            waits=(tm2, t_sil), inc=sV)
                        psG_free[pb] = t_h
                        sil_free[pb] = t_h
                        t_hlast = t_h
                t_ffm_lastread = t_glast_mm
                if pz + 1 < n_pass and ffn_stage < 3:
                    emit_fprep_ln(pz + 1)
                if "hfm" in tap_d and pz == 0:
                    tap_tok_hfm = last_tap[0] = P.do("gpsimd", lambda e: e.dma_start(
                        out=tap_d["hfm"], in_=h_fm[:]),
                        waits=(t_hlast, last_tap[0]), inc=dT, dma=True)

                t_u1last = None
                tu1mm_last = None
                for oc in range(32 if ffn_stage >= 3 else 0):
                    if ln6_q and oc < 16:
                        ln6_q.pop(0)()
                    if pz + 1 < n_pass and oc >= 16 and oc % 2 == 0:
                        emit_fprep_chain(pz + 1, (oc - 16) // 2)
                    po = oc % 2
                    if oc in h1_pre:
                        tok_h1 = h1_pre[oc]
                    else:
                        tok_h1 = P.do("gpsimd", lambda e, oc=oc, po=po:
                                      e.dma_start(
                            out=h1s[po][:],
                            in_=h1_d[oc].rearrange("p a b -> p (a b)")),
                            waits=(h1s_free[po],), inc=dWH[po], dma=True)
                    for nch in range(2):
                        pb = nch
                        tmm = None
                        for jc in range(32):
                            w0 = ()
                            if jc == 0:
                                w0 = (tok_h1, psU_free[pb], t_hlast)
                            tmm = P.do("tensor", lambda e, jc=jc, po=po, nch=nch,
                                       pb=pb: e.matmul(
                                psU[pb][:], h1s[po][:, jc * 128:(jc + 1) * 128],
                                h_fm[:, jc * TT + nch * 512:jc * TT + nch * 512 + 512],
                                start=(jc == 0), stop=(jc == 31),
                                skip_group_check=True),
                                waits=w0, inc=sT if jc == 31 else None)
                        if nch == 1:
                            h1s_free[po] = tmm
                        tu1mm_last = tmm
                        t_u1 = P.do("scalar", lambda e, pb=pb, oc=oc, nch=nch:
                                    e.activation(
                            u1_fm[:, oc * TT + nch * 512:oc * TT + nch * 512 + 512],
                            psU[pb][:], AF.Gelu, scale=sh1c[:, oc:oc + 1]),
                            waits=(tmm, t_u1_lastread, tap_tok_u1fm), inc=sS)
                        psU_free[pb] = t_u1
                        t_u1last = t_u1
                t_hfm_u1read = tu1mm_last
                if pz + 1 < n_pass:
                    emit_fprep_tr(pz + 1)

                t_u2last = None
                tu2mm_last = None
                for oc in range(32 if ffn_stage >= 4 else 0):
                    po = oc % 2
                    tok_h2 = P.do("gpsimd", lambda e, oc=oc, po=po: e.dma_start(
                        out=h1s[po][:], in_=h2_d[oc].rearrange("p a b -> p (a b)")),
                        waits=(h1s_free[po],), inc=dWH[po], dma=True)
                    for nch in range(2):
                        pb = nch
                        tmm = None
                        for jc in range(32):
                            w0 = ()
                            if jc == 0:
                                w0 = (tok_h2, psU_free[pb], t_u1last, t_hfm_u1read)
                            tmm = P.do("tensor", lambda e, jc=jc, po=po, nch=nch,
                                       pb=pb: e.matmul(
                                psU[pb][:], h1s[po][:, jc * 128:(jc + 1) * 128],
                                u1_fm[:, jc * TT + nch * 512:jc * TT + nch * 512 + 512],
                                start=(jc == 0), stop=(jc == 31),
                                skip_group_check=True),
                                waits=w0, inc=sT if jc == 31 else None)
                        if nch == 1:
                            h1s_free[po] = tmm
                        tu2mm_last = tmm
                        t_u2 = P.do("scalar", lambda e, pb=pb, oc=oc, nch=nch:
                                    e.activation(
                            h_fm[:, oc * TT + nch * 512:oc * TT + nch * 512 + 512],
                            psU[pb][:], AF.Gelu, scale=sh2c[:, oc:oc + 1]),
                            waits=(tmm, tap_tok_hfm), inc=sS)
                        psU_free[pb] = t_u2
                        t_u2last = t_u2
                t_u1_lastread = tu2mm_last
                if "u1fm" in tap_d and pz == 0:
                    tap_tok_u1fm = last_tap[0] = P.do("gpsimd", lambda e: e.dma_start(
                        out=tap_d["u1fm"], in_=u1_fm[:]),
                        waits=(t_u1last, last_tap[0]), inc=dT, dma=True)

                psU4 = [psU[0], psU[1], psG1[0], psG1[1]]
                psU4_free = [psU_free[0], psU_free[1],
                             psG1_free[0], psG1_free[1]]
                osc4_free = [osc_free[0], osc_free[1], None, None]
                t_w3_lastread = None
                tlast = None
                pend_tr = [None]
                ln6_tok = [None] * 8
                for dc in range(8 if ffn_stage >= 5 else 0):
                    pw = dc % 2
                    tok_w3 = P.do("gpsimd", lambda e, dc=dc, pw=pw: e.dma_start(
                        out=h1s[pw][:],
                        in_=w3_d[dc].rearrange("p a b -> p (a b)")),
                        waits=(h1s_free[pw],), inc=dWH[pw], dma=True)
                    for nch in range(2):
                        pb = (dc * 2 + nch) % 4
                        tmm = None
                        for jc in range(32):
                            w0 = ()
                            if jc == 0:
                                w0 = (tok_w3, psU4_free[pb], t_u2last)
                            tmm = P.do("tensor", lambda e, jc=jc, pw=pw, nch=nch,
                                       pb=pb: e.matmul(
                                psU4[pb][:],
                                h1s[pw][:, jc * 128:(jc + 1) * 128],
                                h_fm[:, jc * TT + nch * 512:jc * TT + nch * 512 + 512],
                                start=(jc == 0), stop=(jc == 31),
                                skip_group_check=True),
                                waits=w0, inc=sT if jc == 31 else None)
                        if nch == 1:
                            h1s_free[pw] = tmm
                        t_w3_lastread = tmm
                        t_sc3 = P.do("vector", lambda e, pb=pb, dc=dc:
                                     e.tensor_scalar_mul(
                            osc[pb][:], psU4[pb][:], s3c[:, dc:dc + 1]),
                            waits=(tmm, osc4_free[pb]), inc=sV)
                        psU4_free[pb] = t_sc3

                        def mk_tr(pb=pb, dc=dc, nch=nch, t_sc3=t_sc3):
                            nonlocal tlast
                            tl2 = None
                            for qq in range(4):
                                pt = qq
                                tcc = nch * 4 + qq
                                ttr = P.do("tensor", lambda e, pb=pb, qq=qq,
                                           pt=pt: e.transpose(
                                    psT_par[pt][:],
                                    osc[pb][:, qq * 128:(qq + 1) * 128],
                                    ident_bf[:]),
                                    waits=(t_sc3, pst_free[pt]), inc=sT)
                                tcp = P.do("vector", lambda e, tcc=tcc, dc=dc,
                                           pt=pt: e.tensor_copy(
                                    ffn_tm[:, tcc * D + dc * 128:
                                           tcc * D + dc * 128 + 128],
                                    psT_par[pt][:]),
                                    waits=(ttr,
                                           fprep_state["ffntm_free"]
                                           if (dc == 0 and qq == 0) else None,
                                           fbf_free[tcc] if dc == 0 else None),
                                    inc=sV)
                                pst_free[pt] = tcp
                                tl2 = tcp
                            osc4_free[pb] = tl2
                            tlast = tl2

                        if pend_tr[0] is not None:
                            pend_tr[0]()
                        pend_tr[0] = mk_tr
                if pend_tr[0] is not None:
                    pend_tr[0]()
                    pend_tr[0] = None
                tok_alltm = (sV, sV.n, P.gidx)

                def mk_ln6(tcc, t0=t0, tok_alltm=tok_alltm):
                    st = {}
                    src_ = ffn_tm[:, tcc * D:(tcc + 1) * D]
                    nm = stats5[:, 4:5]
                    vr = stats5[:, 5:6]
                    yy = stats5[:, 6:7]
                    tt_ = stats5[:, 7:8]
                    r0 = t0 + tcc * 128

                    def p1():
                        nonlocal xr3_free
                        st["xl"] = P.do("sync", lambda e: e.dma_start(
                            out=xr3[:], in_=x2s_d[r0:r0 + 128, :]),
                            waits=(xr3_free, xr2_free, x2_ready, x2_ready2),
                            inc=dX[1], dma=True)
                        t_r = P.do("vector", lambda e: e.reduce_sum(
                            nm, src_, AX.X), waits=(tok_alltm,), inc=sV)
                        st["nm"] = P.do("vector", lambda e: e.tensor_scalar_mul(
                            nm, nm, -1.0 / D), waits=(t_r,), inc=sV)

                    def p2():
                        nonlocal sq5_free
                        t_c = P.do("vector", lambda e: e.tensor_scalar_add(
                            outt[:], src_, nm), waits=(st["nm"], outt_free,
                                                       sq5_free),
                            inc=sV)
                        t_q = P.do("vector", lambda e: e.tensor_mul(
                            sq5[:], outt[:], outt[:]), waits=(t_c,), inc=sV)
                        t_s = P.do("vector", lambda e: e.reduce_sum(
                            vr, sq5[:], AX.X), waits=(t_q,), inc=sV)
                        sq5_free = t_s
                        st["ve"] = P.do("vector", lambda e: e.tensor_scalar(
                            vr, vr, 1.0 / D, eps_t, ALU.mult, ALU.add),
                            waits=(t_s,), inc=sV)

                    def p3():
                        st["rs"] = rsqrt_v(vr, yy, tt_, (st["ve"],))

                    def p4():
                        nonlocal xr3_free, outt_free
                        t_o = P.do("vector", lambda e: e.scalar_tensor_tensor(
                            outt[:], outt[:], yy, xr3[:], ALU.mult, ALU.add),
                            waits=(st["rs"], st["xl"]), inc=sV)
                        fbf_free[tcc] = t_o
                        xr3_free = t_o
                        t_so = P.do("sync", lambda e: e.dma_start(
                            out=out_d[r0:r0 + 128, :], in_=outt[:]),
                            waits=(t_o,), inc=dS, dma=True)
                        outt_free = t_so
                    return [p1, p2, p3, p4]

                def emit_ln6_final(t0=t0):
                    nonlocal xr3_free, xr2_free, outt_free, sq5_free
                    C = {}

                    def sl(t):
                        g = t % 2
                        b = 4 * g
                        return (stats5[:, b:b + 1], stats5[:, b + 1:b + 2],
                                stats5[:, b + 2:b + 3], stats5[:, b + 3:b + 4])

                    def c_load(t):
                        r0 = t0 + t * 128
                        C[t, "xl"] = P.do("sync", lambda e, r0=r0: e.dma_start(
                            out=xr3[:], in_=x2s_d[r0:r0 + 128, :]),
                            waits=(xr3_free, xr2_free, x2_ready, x2_ready2),
                            inc=dX[1], dma=True)

                    def c_red(t):
                        nm, vr, yy, tt_ = sl(t)
                        src_ = ffn_tm[:, t * D:(t + 1) * D]
                        t_r = P.do("vector", lambda e, nm=nm, src_=src_:
                                   e.reduce_sum(nm, src_, AX.X),
                                   waits=(tok_alltm,), inc=sV)
                        C[t, "nm"] = P.do("vector", lambda e, nm=nm:
                                          e.tensor_scalar_mul(nm, nm, -1.0 / D),
                                          waits=(t_r,), inc=sV)

                    def c_sq(t):
                        nm, vr, yy, tt_ = sl(t)
                        src_ = ffn_tm[:, t * D:(t + 1) * D]
                        C[t, "sq"] = P.do("scalar", lambda e, nm=nm, src_=src_:
                                          e.activation(
                            sq5[:], src_, AF.Square, bias=nm),
                            waits=(C[t, "nm"], C.get((t - 1, "s2")), sq5_free),
                            inc=sS)

                    def c_mid(t):
                        nm, vr, yy, tt_ = sl(t)
                        t_s = P.do("vector", lambda e, vr=vr: e.reduce_sum(
                            vr, sq5[:], AX.X), waits=(C[t, "sq"],), inc=sV)
                        C[t, "s2"] = t_s
                        t_ve = P.do("vector", lambda e, vr=vr: e.tensor_scalar(
                            vr, vr, 1.0 / D, eps_t, ALU.mult, ALU.add),
                            waits=(t_s,), inc=sV)
                        C[t, "rs"] = rsqrt_v(vr, yy, tt_, (t_ve,))

                    def c_out(t):
                        nonlocal xr3_free, xr2_free, outt_free
                        nm, vr, yy, tt_ = sl(t)
                        src_ = ffn_tm[:, t * D:(t + 1) * D]
                        r0 = t0 + t * 128
                        t1 = P.do("vector", lambda e, nm=nm, yy=yy, src_=src_:
                                  e.tensor_scalar(
                            outt[:], src_, nm, yy, ALU.add, ALU.mult),
                            waits=(C[t, "rs"], outt_free), inc=sV)
                        t2 = P.do("vector", lambda e: e.tensor_tensor(
                            outt[:], outt[:], xr3[:], ALU.add),
                            waits=(t1, C[t, "xl"]), inc=sV)
                        fbf_free[t] = t2
                        xr3_free = t2
                        xr2_free = t2
                        t_so = P.do("sync", lambda e, r0=r0: e.dma_start(
                            out=out_d[r0:r0 + 128, :], in_=outt[:]),
                            waits=(t2,), inc=dS, dma=True)
                        outt_free = t_so

                    c_load(0)
                    c_red(0)
                    c_sq(0)
                    for t in range(8):
                        c_mid(t)
                        if t + 1 < 8:
                            c_red(t + 1)
                        c_out(t)
                        if t + 1 < 8:
                            c_load(t + 1)
                            c_sq(t + 1)

                if ffn_stage >= 5:
                    if pz == n_pass - 1 and n_pass == NPASS:
                        emit_ln6_final()
                    else:
                        for tcc in range(8):
                            ln6_q.extend(mk_ln6(tcc))
                if pz == n_pass - 1:
                    while ln6_q:
                        ln6_q.pop(0)()
                t_hfm_lastread = t_w3_lastread
                if ffn_stage >= 5:
                    psU_free[0] = psU4_free[0]
                    psU_free[1] = psU4_free[1]
                    psG1_free[0] = psU4_free[2]
                    psG1_free[1] = psU4_free[3]
                    psG_free[0] = psU4_free[2]
                    psG_free[1] = psU4_free[3]
                    osc_free[0] = osc4_free[0]
                    osc_free[1] = osc4_free[1]
                if "ffntm" in tap_d and pz == 0:
                    tap_tok_ffntm = last_tap[0] = P.do("gpsimd", lambda e: e.dma_start(
                        out=tap_d["ffntm"], in_=ffn_tm[:, 6 * D:8 * D]),
                        waits=((sV, sV.n, P.gidx), last_tap[0]), inc=dT, dma=True)
                    fbf_free[6] = tap_tok_ffntm
                    fbf_free[7] = tap_tok_ffntm

            P.do("vector", None, waits=((dS, dS.n),))
            P.do("gpsimd", None, waits=((dS, dS.n), (dS2, dS2.n), (dC, dC.n), (dT, dT.n)))

        P.emit()
    return nc



def _stage(inputs):
    f = np.ascontiguousarray
    return {
        "wq": f(inputs["L0_attn_Wq"]),
        "wkv": f(inputs["L0_attn_Wkv"]),
        "wout": f(inputs["L0_attn_Wout"]),
        "gq": f(inputs["L0_attn_norm_g"]),
        "go": f(inputs["L0_attn_out_norm_g"]),
        "w12": f(np.concatenate(
            [inputs["L0_ffn_w1_w"].reshape(32, 128, 8, 128).transpose(0, 3, 2, 1),
             inputs["L0_ffn_w2_w"].reshape(32, 128, 8, 128).transpose(0, 3, 2, 1)],
            axis=2)),
        "h1": f(inputs["L0_ffn_h1_w"].reshape(32, 128, 32, 128).transpose(0, 3, 2, 1)),
        "h2": f(inputs["L0_ffn_h2_w"].reshape(32, 128, 32, 128).transpose(0, 3, 2, 1)),
        "w3": f(inputs["L0_ffn_w3_w"].reshape(8, 128, 32, 128).transpose(0, 3, 2, 1)),
        "s1": f(inputs["L0_ffn_w1_s"]),
        "s2": f(inputs["L0_ffn_w2_s"]),
        "sh1": f(inputs["L0_ffn_h1_s"]),
        "sh2": f(inputs["L0_ffn_h2_s"]),
        "s3": f(inputs["L0_ffn_w3_s"]),
    }


def kernel(**inputs):
    np_inputs = {k: np.asarray(v, dtype=np.float32) for k, v in inputs.items()}
    x = np_inputs["L0_attn_x"]
    shared = _stage(np_inputs)
    nc = build(TAPS)
    in_maps = []
    for b in range(8):
        m = dict(shared)
        m["x"] = np.ascontiguousarray(x[b])
        in_maps.append(m)
    res = run_bass_kernel_spmd(nc, in_maps, core_ids=list(range(8)))
    out = np.stack([res.results[b]["out"] for b in range(8)], axis=0)
    if TAPS:
        kernel.tap_results = res.results
    return out


# version 37 (final)
# speedup vs baseline: 1.0705x; 1.0705x over previous
import numpy as np
from contextlib import ExitStack

import concourse.bass as bass
import concourse.mybir as mybir
from concourse.bass_utils import run_bass_kernel_spmd

F32 = mybir.dt.float32
BF16 = mybir.dt.bfloat16
AF = mybir.ActivationFunctionType
ALU = mybir.AluOpType
AX = mybir.AxisListType

N = 4096
D = 1024
H = 8
DH = 64
INNER = H * DH
HID = 4096
TT = 1024
NPASS = N // TT
NTC = N // 128
EPS = 1e-5

TAPS = ()

ENGS = ("sync", "tensor", "scalar", "vector", "gpsimd")


class SemC:
    __slots__ = ("h", "n")

    def __init__(self, h):
        self.h = h
        self.n = 0


class Prog:
    def __init__(self, nc):
        self.nc = nc
        self.ops = {e: [] for e in ENGS}
        self.gidx = 0

    def do(self, eng, fn, waits=(), inc=None, dma=False):
        self.gidx += 1
        wl = []
        for w in waits:
            if w is None:
                continue
            s, v = w[0], w[1]
            if len(w) > 2:
                assert w[2] < self.gidx, "wait on a not-yet-emitted token"
            if v > 0:
                wl.append((s.h, v))
        k = 16 if dma else 1
        tok = None
        if inc is not None:
            inc.n += k
            tok = (inc, inc.n, self.gidx)
        self.ops[eng].append((fn, tuple(wl), (inc.h, k) if inc is not None else None))
        return tok

    def set_barrier_ctx(self, bsc, sems):
        self._bsc = bsc
        self._bar_sems = sems

    def barrier(self, bar=None):
        bsc = self._bsc
        self.do("vector", lambda e: e.memset(bsc[:, 0:1], 0.0), inc=self._sV)
        self.do("scalar", lambda e: e.memzero(bsc[:, 1:2]), inc=self._sS)
        self.do("gpsimd", lambda e: e.memset(bsc[:, 2:3], 0.0), inc=self._sG)
        snap = tuple((s, s.n) for s in self._bar_sems)
        for e in ENGS:
            self.do(e, None, waits=snap)

    def emit(self):
        with self.nc.Block() as block:
            for name in ENGS:
                def mk(n):
                    def body(eng):
                        for fn, waits, inc in self.ops[n]:
                            for h, v in waits:
                                eng.wait_ge(h, v)
                            inst = fn(eng) if fn is not None else None
                            if inc is not None:
                                if inst is None:
                                    inst = eng.nop()
                                inst.then_inc(inc[0], inc[1])
                    return body
                getattr(block, name)(mk(name))


def build(taps=(), phase_limit=3, n_ic=8, n_pair=4, n_pass=NPASS, ffn_stage=5, w3_mode=0):
    nc = bass.Bass("TRN2", target_bir_lowering=False)
    P = Prog(nc)

    x_d = nc.dram_tensor("x", [N, D], F32, kind="ExternalInput").ap()
    wq_d = nc.dram_tensor("wq", [D, INNER], F32, kind="ExternalInput").ap()
    wkv_d = nc.dram_tensor("wkv", [D, 2 * DH], F32, kind="ExternalInput").ap()
    wout_d = nc.dram_tensor("wout", [INNER, D], F32, kind="ExternalInput").ap()
    gq_d = nc.dram_tensor("gq", [D], F32, kind="ExternalInput").ap()
    go_d = nc.dram_tensor("go", [D], F32, kind="ExternalInput").ap()
    w12_d = nc.dram_tensor("w12", [32, 128, 16, 128], F32,
                           kind="ExternalInput").ap()
    h1_d = nc.dram_tensor("h1", [32, 128, 32, 128], F32, kind="ExternalInput").ap()
    h2_d = nc.dram_tensor("h2", [32, 128, 32, 128], F32, kind="ExternalInput").ap()
    w3_d = nc.dram_tensor("w3", [8, 128, 32, 128], F32, kind="ExternalInput").ap()
    s1_d = nc.dram_tensor("s1", [HID], F32, kind="ExternalInput").ap()
    s2_d = nc.dram_tensor("s2", [HID], F32, kind="ExternalInput").ap()
    sh1_d = nc.dram_tensor("sh1", [HID], F32, kind="ExternalInput").ap()
    sh2_d = nc.dram_tensor("sh2", [HID], F32, kind="ExternalInput").ap()
    s3_d = nc.dram_tensor("s3", [D], F32, kind="ExternalInput").ap()
    out_d = nc.dram_tensor("out", [N, D], F32, kind="ExternalOutput").ap()
    x2s_d = nc.dram_tensor("x2s", [N, D], F32,
                           kind="ExternalOutput" if "x2" in taps else "Internal").ap()
    tap_d = {}
    for t in taps:
        if t == "x2":
            continue
        shape = {"xn": [128, D], "qfm": [128, N], "kfm": [128, N],
                 "vones": [128, 32 * 65], "attn": [128, 4 * 512],
                 "hfm": [128, 32 * TT], "u1fm": [128, 32 * TT],
                 "ffntm": [128, 2048]}[t]
        tap_d[t] = nc.dram_tensor("tap_" + t, shape, F32, kind="ExternalOutput").ap()

    last_tap = [None]

    with ExitStack() as root:
        rsb = lambda name, shape, dt: root.enter_context(nc.sbuf_tensor(name, shape, dt))
        sem = lambda name: SemC(root.enter_context(nc.semaphore(name)))

        bar = sem("bar")
        dC = sem("dC")
        dWA = sem("dWA")
        dX = [sem("dX0"), sem("dX1")]
        dF = [sem("dF0"), sem("dF1")]
        dW1 = [sem("dW1a"), sem("dW1b")]
        dW2 = [sem("dW2a"), sem("dW2b")]
        dWH = [sem("dWHa"), sem("dWHb")]
        dS = sem("dS")
        dS2 = sem("dS2")
        dT = sem("dT")
        sT = sem("sT")
        sS = sem("sS")
        sV = sem("sV")
        sG = sem("sG")

        bsc = rsb("bsc", [128, 4], F32)
        PS_t = root.enter_context(nc.psum_tensor("PS", [128, 4096], F32))
        PSf = PS_t.ap()
        PSb = PS_t.ap().bitcast(BF16)
        P._sV, P._sS, P._sG = sV, sS, sG
        P.set_barrier_ctx(bsc, (sT, sS, sV, sG, dC, dWA, dX[0], dX[1],
                                dF[0], dF[1],
                                dW1[0], dW1[1], dW2[0], dW2[1],
                                dWH[0], dWH[1], dS, dS2))
        ident_bf = rsb("ident_bf", [128, 128], BF16)
        ident_f = rsb("ident_f", [32, 32], F32)
        consts = rsb("consts", [128, 12], F32)
        ones64 = rsb("ones64", [33, 64], BF16)
        s1c = rsb("s1c", [128, 32], F32)
        s2c = rsb("s2c", [128, 32], F32)
        sh1c = rsb("sh1c", [128, 32], F32)
        sh2c = rsb("sh2c", [128, 32], F32)
        s3c = rsb("s3c", [128, 8], F32)
        xb0 = [rsb("xb0a", [128, D], F32), rsb("xb0b", [128, D], F32)]
        st0 = rsb("st0", [128, 8], F32)
        fbf0 = rsb("fbf0", [128, 8 * D], BF16)

        neg16 = consts[:, 0:1]
        eps_t = consts[:, 1:2]
        clamp_q = consts[:, 2:3]
        clamp_k = consts[:, 3:4]
        eps2_t = consts[:, 4:5]
        magic4 = consts[:, 5:9]

        t_ms1 = P.do("gpsimd", lambda e: e.memset(ident_bf[:], 0.0), inc=sG)
        P.do("gpsimd", lambda e: e.affine_select(
            out=ident_bf[:], in_=ident_bf[:], compare_op=ALU.not_equal, fill=1.0,
            base=0, pattern=[[-1, 128]], channel_multiplier=1), waits=(t_ms1,),
            inc=sG)
        t_ms2 = P.do("gpsimd", lambda e: e.memset(ident_f[:], 0.0), inc=sG)
        tok_idf = P.do("gpsimd", lambda e: e.affine_select(
            out=ident_f[:], in_=ident_f[:], compare_op=ALU.not_equal, fill=1.0,
            base=0, pattern=[[-1, 32]], channel_multiplier=1), waits=(t_ms2,),
            inc=sG)
        P.do("gpsimd", lambda e: e.memset(consts[:, 0:1], -16.0), inc=sG)
        P.do("gpsimd", lambda e: e.memset(consts[:, 1:2], EPS), inc=sG)
        P.do("gpsimd", lambda e: e.memset(consts[:, 4:5], EPS * (1.0 + EPS)),
             inc=sG)
        I32 = mybir.dt.int32
        P.do("gpsimd", lambda e: e.memset(
            consts[:, 5:9].bitcast(I32), 0x5F3759DF), inc=sG)
        P.do("gpsimd", lambda e: e.memset(consts[:, 2:3], 1.6e13), inc=sG)
        P.do("gpsimd", lambda e: e.memset(consts[:, 3:4], 1e12), inc=sG)
        tok_c = P.do("gpsimd", lambda e: e.memset(ones64[:], 1.0), inc=sG)

        with nc.sbuf_tensor("srows", [32, 128], F32) as srows:
            ps_sc = PSf[:, 0:32]
            t_prev_tr = None
            t_prev_cp = None
            for sd, cols, nrow in ((s1_d, s1c, 32), (s2_d, s2c, 32),
                                   (sh1_d, sh1c, 32), (sh2_d, sh2c, 32),
                                   (s3_d, s3c, 8)):
                tok_l = P.do("sync", lambda e, sd=sd, nrow=nrow: e.dma_start(
                    out=srows[0:nrow, :], in_=sd.rearrange("(a b) -> a b", b=128)),
                    waits=(t_prev_tr,), inc=dC, dma=True)
                t_tr = P.do("tensor", lambda e, nrow=nrow: e.transpose(
                    ps_sc[:, 0:nrow], srows[0:nrow, :], ident_f[0:nrow, 0:nrow]),
                    waits=(tok_l, tok_idf, t_prev_cp), inc=sT)
                t_cp = P.do("vector", lambda e, cols=cols, nrow=nrow: e.tensor_copy(
                    cols[:, 0:nrow], ps_sc[:, 0:nrow]), waits=(t_tr,), inc=sV)
                t_prev_tr = t_tr
                t_prev_cp = t_cp
            tok_scales = t_cp

        P.barrier(bar)

        def ln(src, dst, gain_b, tmp, sq_buf, stt, wts, sqs=None, sqb_c=None,
               s_legs=False):
            nm = stt[:, 0:1]
            vr = stt[:, 1:2]
            yy = stt[:, 2:3]
            tt_ = stt[:, 3:4]
            t_r = P.do("vector", lambda e: e.reduce_sum(nm, src, AX.X), waits=wts,
                       inc=sV)
            t_nm = P.do("vector", lambda e: e.tensor_scalar_mul(nm, nm, -1.0 / D),
                        waits=(t_r,), inc=sV)
            if s_legs:
                t_sq = P.do("scalar", lambda e: e.activation(
                    sq_buf, src, AF.Square, bias=nm), waits=(t_nm,), inc=sS)
                t_s2 = P.do("vector", lambda e: e.reduce_sum(vr, sq_buf, AX.X),
                            waits=(t_sq,), inc=sV)
                t_sd = P.do("scalar", lambda e: e.activation(
                    yy, vr, AF.Sqrt, bias=eps_t if sqb_c is None else sqb_c,
                    scale=(1.0 / D) if sqs is None else sqs),
                    waits=(t_s2,), inc=sS)
                t_rs = P.do("vector", lambda e: e.reciprocal(tt_, yy),
                            waits=(t_sd,), inc=sV)
                if gain_b is None:
                    return P.do("vector", lambda e: e.tensor_scalar(
                        dst, src, nm, tt_, ALU.add, ALU.mult),
                        waits=(t_rs,), inc=sV)
                t_ap = P.do("vector", lambda e: e.tensor_scalar(
                    tmp, src, nm, tt_, ALU.add, ALU.mult), waits=(t_rs,), inc=sV)
                return P.do("vector", lambda e: e.tensor_tensor(
                    dst, tmp, gain_b, ALU.mult), waits=(t_ap,), inc=sV)
            t_c = P.do("vector", lambda e: e.tensor_scalar_add(sq_buf, src, nm),
                       waits=(t_nm,), inc=sV)
            sq_dst = tmp if gain_b is not None else src
            t_q = P.do("vector", lambda e: e.tensor_mul(sq_dst, sq_buf, sq_buf),
                       waits=(t_c,), inc=sV)
            t_s2 = P.do("vector", lambda e: e.reduce_sum(vr, sq_dst, AX.X),
                        waits=(t_q,), inc=sV)
            t_ve = P.do("vector", lambda e: e.tensor_scalar(
                vr, vr, (1.0 / D) if sqs is None else sqs,
                eps_t if sqb_c is None else sqb_c, ALU.mult, ALU.add),
                waits=(t_s2,), inc=sV)
            t_rs = rsqrt_v(vr, yy, tt_, (t_ve,))
            if gain_b is None:
                return P.do("vector", lambda e: e.tensor_scalar_mul(
                    dst, sq_buf, yy), waits=(t_rs,), inc=sV)
            return P.do("vector", lambda e: e.scalar_tensor_tensor(
                dst, sq_buf, yy, gain_b, ALU.mult, ALU.mult),
                waits=(t_rs,), inc=sV)

        def rsqrt_v(v, y, t, wts):
            I32 = mybir.dt.int32
            tk = P.do("vector", lambda e: e.tensor_single_scalar(
                y.bitcast(I32), v.bitcast(I32), 1, ALU.arith_shift_right),
                waits=wts, inc=sV)
            kk = y.shape[1]
            tk = P.do("vector", lambda e, kk=kk: e.tensor_sub(
                y.bitcast(I32), magic4[:, 0:kk].bitcast(I32),
                y.bitcast(I32)), waits=(tk,), inc=sV)
            for _ in range(2):
                if kk == 1:
                    tk = P.do("vector", lambda e: e.scalar_tensor_tensor(
                        t, y, v, y, ALU.mult, ALU.mult), waits=(tk,), inc=sV)
                else:
                    tk = P.do("vector", lambda e: e.tensor_mul(t, y, y),
                              waits=(tk,), inc=sV)
                    tk = P.do("vector", lambda e: e.tensor_mul(t, t, v),
                              waits=(tk,), inc=sV)
                tk = P.do("vector", lambda e: e.tensor_scalar(
                    t, t, -0.5, 1.5, ALU.mult, ALU.add), waits=(tk,), inc=sV)
                tk = P.do("vector", lambda e: e.tensor_mul(y, y, t),
                          waits=(tk,), inc=sV)
            return tk

        fp0_toks = [None] * 8
        xb0_free = [None, None]
        sc0_free = [None]
        tok_x2p0 = {}

        def mk_fprep0_pieces(t):
            p = t % 2
            r0 = t * 128
            nm = st0[:, p * 4:p * 4 + 1]
            vr = st0[:, p * 4 + 1:p * 4 + 2]
            yy = st0[:, p * 4 + 2:p * 4 + 3]
            tt_ = st0[:, p * 4 + 3:p * 4 + 4]
            st = {}

            def pa():
                st["l"] = P.do("sync", lambda e, p=p, r0=r0: e.dma_start(
                    out=xb0[p][:], in_=x2s_d[r0:r0 + 128, :]),
                    waits=tok_x2p0[2 if t < 4 else 4] + (xb0_free[p],),
                    inc=dF[p], dma=True)
                t_r = P.do("vector", lambda e, p=p, nm=nm: e.reduce_sum(
                    nm, xb0[p][:], AX.X), waits=(st["l"],), inc=sV)
                st["nm"] = P.do("vector", lambda e, nm=nm: e.tensor_scalar_mul(
                    nm, nm, -1.0 / D), waits=(t_r,), inc=sV)

            def pb():
                t_c = P.do("vector", lambda e, p=p, nm=nm: e.tensor_scalar_add(
                    xb0[p][:], xb0[p][:], nm), waits=(st["nm"],), inc=sV)
                st["q"] = P.do("vector", lambda e, p=p: e.tensor_mul(
                    sqb[0][:], xb0[p][:], xb0[p][:]),
                    waits=(t_c, sc0_free[0]), inc=sV)

            def pc():
                t_s = P.do("vector", lambda e, vr=vr: e.reduce_sum(
                    vr, sqb[0][:], AX.X), waits=(st["q"],), inc=sV)
                sc0_free[0] = t_s
                t_ve = P.do("vector", lambda e, vr=vr: e.tensor_scalar(
                    vr, vr, 1.0 / D, eps_t, ALU.mult, ALU.add),
                    waits=(t_s,), inc=sV)
                st["rs"] = rsqrt_v(vr, yy, tt_, (t_ve,))

            def pd():
                t_f = P.do("vector", lambda e, p=p, t=t, yy=yy:
                           e.tensor_scalar_mul(
                    fbf0[:, t * D:(t + 1) * D], xb0[p][:], yy),
                    waits=(st["rs"],), inc=sV)
                xb0_free[p] = t_f
                fp0_toks[t] = t_f
            return [pa, pb, pc, pd]

        with ExitStack() as ascope:
            asb = lambda name, shape, dt: ascope.enter_context(nc.sbuf_tensor(name, shape, dt))

            gq_b = asb("gq_b", [128, D], F32)
            go_b = asb("go_b", [128, D], F32)
            wq_bf = asb("wq_bf", [128, 8 * INNER], BF16)
            wkv_bf = asb("wkv_bf", [128, 8 * 128], BF16)
            wout_bf = asb("wout_bf", [128, 4 * D], BF16)
            q_fma = asb("q_fma", [128, 4 * N], BF16)
            q_fm = [q_fma[:, c * N:(c + 1) * N] for c in range(4)]
            k_fm2 = asb("k_fm2", [128, N], BF16)
            v_ones = asb("v_ones", [128, 32 * 65], BF16)
            xt = [asb(f"xt{i}", [128, D], F32) for i in range(2)]
            sqb = [asb(f"sqb{i}", [128, D], F32) for i in range(2)]
            stats = asb("stats", [128, 64], F32)
            xn_bf = [asb(f"xn_bf{i}", [128, D], BF16) for i in range(2)]
            xn_fmt = [asb(f"xn_fmt{i}", [128, D], BF16) for i in range(2)]
            q4_bf = [asb(f"q4_bf{i}", [128, INNER], BF16) for i in range(2)]
            sq_q = asb("sq_q", [128, INNER], F32)
            sq_k = [asb(f"sq_k{i}", [128, DH], F32) for i in range(2)]
            k4_bf = [asb(f"k4_bf{i}", [128, DH], BF16) for i in range(2)]
            pT = [asb(f"pT{i}", [128, 1024], BF16) for i in range(4)]
            attn_ic = [asb(f"attn_ic{i}", [128, 4 * 512], BF16) for i in range(2)]
            oa_sb = [asb(f"oa_sb{i}", [65, 512], F32) for i in range(2)]
            ob_sb = [asb(f"ob_sb{i}", [65, 512], F32) for i in range(2)]
            rrow2 = asb("rrow2", [33, 512], F32)
            rrow2b = asb("rrow2b", [33, 512], BF16)
            bc_sb = asb("bc_sb", [64, 512], F32)
            ot_all = asb("ot_all", [128, 4 * D], F32)
            cent = asb("cent", [128, 4 * D], F32)
            t3_all = asb("t3_all", [128, 4 * D], F32)
            xr_f = wq_bf.ap().bitcast(F32)
            xr = [xr_f[:, 0:D], xr_f[:, D:2 * D]]
            x2t = xt

            P.do("sync", lambda e: e.dma_start(
                out=gq_b[:], in_=gq_d.rearrange("(o d) -> o d", o=1).broadcast_to([128, D])),
                inc=dC, dma=True)
            tok_gain = P.do("sync", lambda e: e.dma_start(
                out=go_b[:], in_=go_d.rearrange("(o d) -> o d", o=1).broadcast_to([128, D])),
                inc=dC, dma=True)
            tok_gq = tok_go = tok_gain
            P.do("gpsimd", lambda e: e.dma_start(
                out=wq_bf[:].rearrange("p (c q) -> p c q", c=8),
                in_=wq_d.rearrange("(c p) q -> p c q", p=128)),
                inc=dWA, dma=True)
            P.do("gpsimd", lambda e: e.dma_start(
                out=wkv_bf[:].rearrange("p (c q) -> p c q", c=8),
                in_=wkv_d.rearrange("(c p) q -> p c q", p=128)),
                inc=dWA, dma=True)
            tok_wall = P.do("gpsimd", lambda e: e.dma_start(
                out=wout_bf[:].rearrange("p (c q) -> p c q", c=4),
                in_=wout_d.rearrange("(c p) q -> p c q", p=128)),
                inc=dWA, dma=True)
            tok_wq = tok_wkv = tok_wout = tok_wall
            P.do("vector", lambda e: e.memset(rrow2[:], 1.0))
            tok_vo = None
            for j in range(32):
                tok_vo = P.do("vector", lambda e, j=j: e.memset(
                    v_ones[:, j * 65 + 64:j * 65 + 65], 1.0), inc=sV)

            if True:
                q_ps = [PSf[:, 0:512], PSf[:, 5 * 512:6 * 512]]
                kv_ps = [PSf[:, 512:640], PSf[:, 6 * 512:6 * 512 + 128]]
                tr_ps = [PSb[:, 2 * 1024:3 * 1024],
                         PSb[:, 7 * 1024:8 * 1024]]
                trq_ps = PSb[:, 3 * 1024:3 * 1024 + 512]
                trk_ps = PSb[0:64, 4 * 1024:4 * 1024 + 128]

                xt_free = [None, None]
                trps_free = [None, None]
                trqps_free = None
                trkps_free = None

                xnbf_free = [None, None]
                xnfmt_free = [None, None]
                q4_free = [None, None]
                k4_free = [None, None]
                qps_free = [tok_scales, tok_scales]
                kvps_free = [tok_scales, None]
                sqq_free = [None, None]
                sqk_free = [None, None]

                A_ttr = [None, None]
                Lst = [{}, {}]

                def lnA1(tc):
                    p = tc % 2
                    r0 = tc * 128
                    nm = stats[:, p * 32:p * 32 + 1]
                    tok_x = P.do("sync", lambda e, p=p, r0=r0: e.dma_start(
                        out=xt[p][:], in_=x_d[r0:r0 + 128, :]),
                        waits=(xt_free[p],), inc=dX[p], dma=True)
                    t_r = P.do("vector", lambda e, nm=nm: e.reduce_sum(
                        nm, xt[p][:], AX.X),
                        waits=(tok_x, tok_gq, xnbf_free[p]), inc=sV)
                    t_nm = P.do("vector", lambda e, nm=nm: e.tensor_scalar_mul(
                        nm, nm, -1.0 / D), waits=(t_r,), inc=sV)
                    Lst[p]["sq"] = P.do("scalar", lambda e, p=p, nm=nm:
                                        e.activation(
                        sqb[p][:], xt[p][:], AF.Square, bias=nm),
                        waits=(t_nm,), inc=sS)

                def lnA2(tc):
                    p = tc % 2
                    vr = stats[:, p * 32 + 1:p * 32 + 2]
                    sd = stats[:, p * 32 + 2:p * 32 + 3]
                    t_s2 = P.do("vector", lambda e, p=p, vr=vr: e.reduce_sum(
                        vr, sqb[p][:], AX.X), waits=(Lst[p]["sq"],), inc=sV)
                    Lst[p]["sd"] = P.do("scalar", lambda e, vr=vr, sd=sd:
                                        e.activation(
                        sd, vr, AF.Sqrt, bias=eps2_t, scale=(1.0 + EPS) / D),
                        waits=(t_s2,), inc=sS)

                def lnA3(tc):
                    p = tc % 2
                    nm = stats[:, p * 32:p * 32 + 1]
                    sd = stats[:, p * 32 + 2:p * 32 + 3]
                    rstd = stats[:, p * 32 + 3:p * 32 + 4]
                    t_rs = P.do("vector", lambda e, sd=sd, rstd=rstd:
                                e.reciprocal(rstd, sd),
                                waits=(Lst[p]["sd"],), inc=sV)
                    t_ap = P.do("vector", lambda e, p=p, nm=nm, rstd=rstd:
                                e.tensor_scalar(
                        sqb[p][:], xt[p][:], nm, rstd, ALU.add, ALU.mult),
                        waits=(t_rs,), inc=sV)
                    tok_xn = P.do("vector", lambda e, p=p: e.tensor_tensor(
                        xn_bf[p][:], sqb[p][:], gq_b[:], ALU.mult),
                        waits=(t_ap,), inc=sV)
                    xt_free[p] = tok_xn
                    ttr = None
                    for dc in range(8):
                        ttr = P.do("tensor", lambda e, dc=dc, p=p: e.transpose(
                            tr_ps[p][:, dc * 128:(dc + 1) * 128],
                            xn_bf[p][:, dc * 128:(dc + 1) * 128], ident_bf[:]),
                            waits=(tok_xn, trps_free[p]) if dc == 0 else (),
                            inc=sT if dc == 7 else None)
                    xnbf_free[p] = ttr
                    A_ttr[p] = ttr

                Bst = [{}, {}]

                def stageB_proj(tc):
                    p = tc % 2
                    tok_fmt = P.do("scalar", lambda e, p=p: e.activation(
                        xn_fmt[p][:], tr_ps[p][:], AF.Copy),
                        waits=(A_ttr[p], xnfmt_free[p]), inc=sS)
                    trps_free[p] = tok_fmt
                    tok_qmm = None
                    for dc in range(8):
                        tok_qmm = P.do("tensor", lambda e, dc=dc, p=p: e.matmul(
                            q_ps[p][:], xn_fmt[p][:, dc * 128:(dc + 1) * 128],
                            wq_bf[:, dc * INNER:(dc + 1) * INNER],
                            start=(dc == 0), stop=(dc == 7), skip_group_check=True),
                            waits=(tok_fmt, tok_wq, qps_free[p]) if dc == 0 else (),
                            inc=sT if dc == 7 else None)
                    tok_kvmm = None
                    for dc in range(8):
                        tok_kvmm = P.do("tensor", lambda e, dc=dc, p=p: e.matmul(
                            kv_ps[p][:], xn_fmt[p][:, dc * 128:(dc + 1) * 128],
                            wkv_bf[:, dc * 128:(dc + 1) * 128],
                            start=(dc == 0), stop=(dc == 7), skip_group_check=True),
                            waits=(tok_wkv, kvps_free[p]) if dc == 0 else (),
                            inc=sT if dc == 7 else None)
                    xnfmt_free[p] = tok_kvmm
                    Bst[p]["qmm"] = tok_qmm
                    Bst[p]["kvmm"] = tok_kvmm

                def epiS(tc):
                    p = tc % 2
                    Bst[p]["sqq"] = P.do("scalar", lambda e, p=p: e.activation(
                        sq_q[:], q_ps[p][:], AF.Square),
                        waits=(Bst[p]["qmm"], sqq_free[0]), inc=sS)
                    Bst[p]["sqk"] = P.do("scalar", lambda e, p=p: e.activation(
                        sq_k[p][:], kv_ps[p][:, 0:DH], AF.Square),
                        waits=(Bst[p]["kvmm"], sqk_free[p]), inc=sS)

                def epiV1(tc):
                    p = tc % 2
                    st_q = stats[:, p * 32 + 8:p * 32 + 16]
                    st_k1 = stats[:, p * 32 + 16:p * 32 + 17]
                    t_ssq = P.do("vector", lambda e, st_q=st_q: e.reduce_sum(
                        st_q, sq_q[:].rearrange("p (h d) -> p h d", h=8),
                        AX.X), waits=(Bst[p]["sqq"],), inc=sV)
                    sqq_free[0] = t_ssq
                    t_ssk = P.do("vector", lambda e, p=p, st_k1=st_k1: e.reduce_sum(
                        st_k1, sq_k[p][:], AX.X), waits=(Bst[p]["sqk"],), inc=sV)
                    sqk_free[p] = t_ssk
                    Bst[p]["ssq"] = t_ssq
                    Bst[p]["ssk"] = t_ssk

                def epiR(tc):
                    nonlocal trqps_free, trkps_free
                    p = tc % 2
                    r0 = tc * 128
                    st_q = stats[:, p * 32 + 8:p * 32 + 16]
                    st_k1 = stats[:, p * 32 + 16:p * 32 + 17]
                    st_qk = stats[:, p * 32 + 8:p * 32 + 17]
                    t_nqk = P.do("scalar", lambda e, st_qk=st_qk:
                                 e.activation(st_qk, st_qk, AF.Sqrt),
                                 waits=(Bst[p]["ssq"], Bst[p]["ssk"]), inc=sS)
                    t_rqk = P.do("vector", lambda e, st_qk=st_qk: e.reciprocal(
                        st_qk, st_qk), waits=(t_nqk,), inc=sV)
                    t_fac = P.do("vector", lambda e, st_q=st_q: e.tensor_scalar(
                        st_q, st_q, clamp_q, 4.0,
                        ALU.min, ALU.mult), waits=(t_rqk,), inc=sV)
                    t_fk = P.do("vector", lambda e, st_k1=st_k1: e.tensor_scalar(
                        st_k1, st_k1, clamp_k, 4.0,
                        ALU.min, ALU.mult), inc=sV)
                    tq4 = P.do("vector", lambda e, p=p, st_q=st_q: e.tensor_tensor(
                        q4_bf[p][:].rearrange("p (h d) -> p h d", h=8),
                        q_ps[p][:].rearrange("p (h d) -> p h d", h=8),
                        st_q.rearrange("p (h o) -> p h o", o=1)
                            .broadcast_to([128, 8, DH]),
                        ALU.mult), waits=(t_fac, q4_free[p]), inc=sV)
                    qps_free[p] = tq4
                    ttrq = None
                    for c in range(4):
                        ttrq = P.do("tensor", lambda e, c=c, p=p: e.transpose(
                            trq_ps[:, c * 128:(c + 1) * 128],
                            q4_bf[p][:, c * 128:(c + 1) * 128], ident_bf[:]),
                            waits=(tq4, trqps_free) if c == 0 else (),
                            inc=sT if c == 3 else None)
                    q4_free[p] = ttrq
                    tcq = P.do("scalar", lambda e, r0=r0: e.activation(
                        q_fma.rearrange("p (c n) -> p c n", c=4)[:, :, r0:r0 + 128],
                        trq_ps[:].rearrange("p (c w) -> p c w", c=4), AF.Copy),
                        waits=(ttrq,), inc=sS)
                    trqps_free = tcq
                    tk4 = P.do("vector", lambda e, p=p, st_k1=st_k1:
                               e.tensor_scalar_mul(
                        k4_bf[p][:], kv_ps[p][:, 0:DH], st_k1),
                        waits=(t_fk, k4_free[p]), inc=sV)
                    t_trk = P.do("tensor", lambda e, p=p: e.transpose(
                        trk_ps[:], k4_bf[p][:], ident_bf[:]),
                        waits=(tk4, trkps_free), inc=sT)
                    k4_free[p] = t_trk
                    P.do("scalar", lambda e, r0=r0: e.activation(
                        k_fm2[0:64, r0:r0 + 128], trk_ps[:], AF.Copy),
                        waits=(t_trk,))
                    tkc2 = P.do("scalar", lambda e, r0=r0: e.activation(
                        k_fm2[64:128, r0:r0 + 128], trk_ps[:], AF.Copy), inc=sS)
                    trkps_free = tkc2
                    tvv = P.do("scalar", lambda e, tc=tc, p=p: e.activation(
                        v_ones[:, tc * 65:tc * 65 + 64], kv_ps[p][:, DH:2 * DH],
                        AF.Copy),
                        waits=(tok_vo,) if tc == 0 else (), inc=sS)
                    kvps_free[p] = tvv

                for w in (0, 1):
                    if w < NTC:
                        lnA1(w)
                        lnA2(w)
                        lnA3(w)
                stageB_proj(0)
                for tc in range(NTC):
                    if tc + 1 < NTC:
                        stageB_proj(tc + 1)
                    if tc + 2 < NTC:
                        lnA1(tc + 2)
                        lnA2(tc + 2)
                    epiS(tc)
                    epiV1(tc)
                    if tc + 2 < NTC:
                        lnA3(tc + 2)
                    epiR(tc)

                tok_p12_V = (sV, sV.n, P.gidx)
                tok_p12_T = (sT, sT.n, P.gidx)

            if "xn" in tap_d:
                last_tap[0] = P.do("gpsimd", lambda e: e.dma_start(
                    out=tap_d["xn"], in_=xn_bf[1][:]),
                    waits=(tok_p12_V, last_tap[0]), inc=dT, dma=True)
            if "qfm" in tap_d:
                last_tap[0] = P.do("gpsimd", lambda e: e.dma_start(
                    out=tap_d["qfm"], in_=q_fm[0][:]),
                    waits=(tok_p12_V, last_tap[0]), inc=dT, dma=True)
            if "kfm" in tap_d:
                last_tap[0] = P.do("gpsimd", lambda e: e.dma_start(
                    out=tap_d["kfm"], in_=k_fm2[:]),
                    waits=(tok_p12_V, last_tap[0]), inc=dT, dma=True)
            if "vones" in tap_d:
                last_tap[0] = P.do("gpsimd", lambda e: e.dma_start(
                    out=tap_d["vones"], in_=v_ones[:]),
                    waits=(tok_p12_V, last_tap[0]), inc=dT, dma=True)

            P.barrier(bar)

            if True:
                S_AB = [PSf[:, 0:1024], PSf[:, 1024:2048]]
                O_A = PSf[0:65, 2048:2560]
                O_B = PSf[0:65, 2560:3072]
                bc_ps = PSf[0:64, 3072:3584]
                op_ps = PSf[:, 3584:4096]

                sab_free = [None, None]
                pT_free = [None, None, None, None]
                o_free = None
                bc_free = None
                rrow_free = None
                oa_free = [None, None]
                attn_free = [None, None]
                opps_free = None
                otall_free = None
                x2t_free = [None, None]
                xr_free = [None, None]
                pending_norm = [[]]
                norm_last = [None]
                bcsb_free = [None]
                sched = {}

                def sched_add(gp, jt, fn):
                    sched.setdefault((gp, jt), []).append(fn)

                def emit_norm_recip(ic, pair, tva_, tvb_):
                    nonlocal rrow_free
                    pp2 = pair % 2
                    cpa = P.do("vector", lambda e, pp2=pp2: e.tensor_copy(
                        rrow2[0:1, :], oa_sb[pp2][64:65, :]),
                        waits=(tva_, rrow_free), inc=sV)
                    cpb = P.do("vector", lambda e, pp2=pp2: e.tensor_copy(
                        rrow2[32:33, :], ob_sb[pp2][64:65, :]),
                        waits=(tvb_,), inc=sV)
                    t_rc = P.do("vector", lambda e: e.reciprocal(
                        rrow2[:], rrow2[:]), waits=(cpb,), inc=sV)
                    t_cb = P.do("vector", lambda e: e.tensor_copy(
                        rrow2b[:], rrow2[:]), waits=(t_rc,), inc=sV)
                    return t_cb

                def make_norm_bc(ic, pair, t_rc, half):
                    pp2 = pair % 2
                    aic = ic % 2

                    def emit():
                        nonlocal bc_free, rrow_free
                        row0 = 0 if half == 0 else 32
                        o_sb = oa_sb if half == 0 else ob_sb
                        prow = 0 if half == 0 else 64
                        tb = P.do("tensor", lambda e, row0=row0: e.matmul(
                            bc_ps[:], ones64[row0:row0 + 1, :],
                            rrow2b[row0:row0 + 1, :],
                            start=True, stop=True, skip_group_check=True),
                            waits=(t_rc, bc_free, tok_c), inc=sT)
                        if half == 1:
                            rrow_free = tb
                        tc_ = P.do("vector", lambda e: e.tensor_copy(
                            bc_sb[:], bc_ps[:]), waits=(tb, bcsb_free[0]), inc=sV)
                        bc_free = tc_
                        tt = P.do("vector", lambda e, pp2=pp2, aic=aic, pair=pair,
                                  prow=prow, o_sb=o_sb: e.tensor_tensor(
                            attn_ic[aic][prow:prow + 64,
                                         pair * 512:(pair + 1) * 512],
                            o_sb[pp2][0:64, :], bc_sb[:], ALU.mult),
                            waits=(tc_, attn_free[aic] if half == 0 else None),
                            inc=sV)
                        bcsb_free[0] = tt
                        if half == 1:
                            oa_free[pp2] = tt
                        return tt
                    return emit

                norm_tok = {}

                def emit_phase4(ic, flush=False):
                    aic = ic % 2
                    i0 = ic * 512
                    m3 = stats[:, 20:24]
                    r3 = stats[:, 24:28]
                    m4 = stats[:, 28:32]
                    r4 = stats[:, 52:56]
                    tok_xr = [None] * 4
                    op2_free = [None]
                    state = {}

                    def mk_group(tcc, nh):
                        def emit():
                            nonlocal opps_free, otall_free
                            if tcc == 0 and nh == 0:
                                for t2 in range(2):
                                    r0 = i0 + t2 * 128
                                    tok_xr[t2] = P.do("sync",
                                        lambda e, pp=t2, r0=r0: e.dma_start(
                                            out=xr[pp][:],
                                            in_=x_d[r0:r0 + 128, :]),
                                        waits=(xr_free[t2],
                                               tok_p12_T if ic == 0 else None),
                                        inc=dX[t2], dma=True)
                            gi = tcc * 2 + nh
                            if flush and gi % 2 == 1:
                                ps = PSf[:, 0:512]
                                ps_free = op2_free[0]
                            else:
                                ps = op_ps
                                ps_free = opps_free
                            tmm = None
                            for c in range(4):
                                w0 = ()
                                if c == 0:
                                    w0 = (norm_last[0], tok_wout, ps_free,
                                          otall_free if (tcc == 0 and nh == 0)
                                          else None)
                                tmm = P.do("tensor", lambda e, c=c, nh=nh,
                                           tcc=tcc, aic=aic, ps=ps: e.matmul(
                                    ps[:],
                                    attn_ic[aic][:, c * 512 + tcc * 128:
                                                 c * 512 + tcc * 128 + 128],
                                    wout_bf[:, c * D + nh * 512:
                                            c * D + nh * 512 + 512],
                                    start=(c == 0), stop=(c == 3),
                                    skip_group_check=True),
                                    waits=w0, inc=sT if c == 3 else None)
                                if tcc == 3 and nh == 1 and c == 3:
                                    attn_free[aic] = tmm
                            t_ot = P.do("vector", lambda e, nh=nh, tcc=tcc,
                                        ps=ps: e.tensor_copy(
                                ot_all[:, tcc * D + nh * 512:
                                       tcc * D + nh * 512 + 512], ps[:]),
                                waits=(tmm,), inc=sV)
                            if flush and gi % 2 == 1:
                                op2_free[0] = t_ot
                            else:
                                opps_free = t_ot
                            state["t_ot"] = t_ot
                        return emit

                    def ln3_a():
                        t_r3 = P.do("vector", lambda e: e.reduce_sum(
                            m3, ot_all[:].rearrange("p (t d) -> p t d", t=4),
                            AX.X), waits=(state["t_ot"],), inc=sV)
                        state["t_m3"] = P.do("vector", lambda e:
                                             e.tensor_scalar_mul(
                            m3, m3, -1.0 / D), waits=(t_r3,), inc=sV)

                    def ln3_b():
                        t_c3 = None
                        for tcc in range(4):
                            t_c3 = P.do("vector", lambda e, tcc=tcc:
                                        e.tensor_scalar_add(
                                cent[:, tcc * D:(tcc + 1) * D],
                                ot_all[:, tcc * D:(tcc + 1) * D],
                                m3[:, tcc:tcc + 1]),
                                waits=(state["t_m3"],) if tcc == 0 else (),
                                inc=sV if tcc == 3 else None)
                        state["t_c3"] = t_c3

                    def ln3_c():
                        if flush:
                            state["t_sq3"] = P.do("scalar", lambda e:
                                                  e.activation(
                                t3_all[:], cent[:], AF.Square),
                                waits=(state["t_c3"],), inc=sS)
                        else:
                            state["t_sq3"] = P.do("vector", lambda e:
                                                  e.tensor_mul(
                                t3_all[:], cent[:], cent[:]),
                                waits=(state["t_c3"],), inc=sV)

                    def ln3_d():
                        t_s23 = P.do("vector", lambda e: e.reduce_sum(
                            r3, t3_all[:].rearrange("p (t d) -> p t d", t=4),
                            AX.X), waits=(state["t_sq3"],), inc=sV)
                        state["t_ve3"] = P.do("vector", lambda e: e.tensor_scalar(
                            r3, r3, 1.0 / D, eps_t, ALU.mult, ALU.add),
                            waits=(t_s23,), inc=sV)

                    def ln3_e():
                        y3 = stats[:, 56:60]
                        state["y3"] = y3
                        state["t_rs3"] = rsqrt_v(r3, y3, stats[:, 60:64],
                                                 (state["t_ve3"],))

                    def ln3_f():
                        nonlocal otall_free
                        y3 = state["y3"]
                        t_a3 = None
                        for tcc in range(4):
                            t_a3 = P.do("vector", lambda e, tcc=tcc, y3=y3:
                                        e.scalar_tensor_tensor(
                                t3_all[:, tcc * D:(tcc + 1) * D],
                                cent[:, tcc * D:(tcc + 1) * D],
                                y3[:, tcc:tcc + 1], go_b[:], ALU.mult,
                                ALU.mult),
                                waits=(state["t_rs3"], tok_go)
                                      if tcc == 0 else (),
                                inc=sV if tcc == 3 else None)
                        state["t_a3"] = t_a3

                    def ln4_a():
                        nonlocal otall_free
                        t_r4 = P.do("vector", lambda e: e.reduce_sum(
                            m4, t3_all[:].rearrange("p (t d) -> p t d", t=4),
                            AX.X), waits=(state["t_a3"],), inc=sV)
                        t_m4 = P.do("vector", lambda e: e.tensor_scalar_mul(
                            m4, m4, -1.0 / D), waits=(t_r4,), inc=sV)
                        t_c4 = None
                        for tcc in range(4):
                            t_c4 = P.do("vector", lambda e, tcc=tcc:
                                        e.tensor_scalar_add(
                                cent[:, tcc * D:(tcc + 1) * D],
                                t3_all[:, tcc * D:(tcc + 1) * D],
                                m4[:, tcc:tcc + 1]),
                                waits=(t_m4,) if tcc == 0 else (),
                                inc=sV if tcc == 3 else None)
                        otall_free = t_c4
                        state["t_c4"] = t_c4

                    def ln4_b():
                        if flush:
                            state["t_sq4"] = P.do("scalar", lambda e:
                                                  e.activation(
                                t3_all[:], cent[:], AF.Square),
                                waits=(state["t_c4"],), inc=sS)
                        else:
                            state["t_sq4"] = P.do("vector", lambda e:
                                                  e.tensor_mul(
                                t3_all[:], cent[:], cent[:]),
                                waits=(state["t_c4"],), inc=sV)

                    def ln4_c():
                        t_s24 = P.do("vector", lambda e: e.reduce_sum(
                            r4, t3_all[:].rearrange("p (t d) -> p t d", t=4),
                            AX.X), waits=(state["t_sq4"],), inc=sV)
                        t_ve4 = P.do("vector", lambda e: e.tensor_scalar(
                            r4, r4, 1.0 / D, eps_t, ALU.mult, ALU.add),
                            waits=(t_s24,), inc=sV)
                        state["y4"] = stats[:, 56:60]
                        state["t_rs4"] = rsqrt_v(r4, state["y4"],
                                                 stats[:, 60:64], (t_ve4,))

                    def ln4_d():
                        t_rs4 = state["t_rs4"]
                        y4 = state["y4"]
                        for tcc in range(4):
                            r0 = i0 + tcc * 128
                            pp = tcc % 2
                            t_x2 = P.do("vector", lambda e, tcc=tcc, pp=pp,
                                        y4=y4: e.scalar_tensor_tensor(
                                x2t[pp][:], cent[:, tcc * D:(tcc + 1) * D],
                                y4[:, tcc:tcc + 1], xr[pp][:], ALU.mult,
                                ALU.add),
                                waits=(t_rs4 if tcc == 0 else None,
                                       tok_xr[tcc], x2t_free[pp]), inc=sV)
                            xr_free[pp] = t_x2
                            if tcc < 2:
                                r2 = i0 + (tcc + 2) * 128
                                tok_xr[tcc + 2] = P.do("sync",
                                    lambda e, pp=pp, r2=r2: e.dma_start(
                                        out=xr[pp][:],
                                        in_=x_d[r2:r2 + 128, :]),
                                    waits=(t_x2,), inc=dX[pp], dma=True)
                            t_st = P.do("sync", lambda e, r0=r0, pp=pp:
                                        e.dma_start(
                                out=x2s_d[r0:r0 + 128, :], in_=x2t[pp][:]),
                                waits=(t_x2,), inc=dS if pp == 0 else dS2,
                                dma=True)
                            x2t_free[pp] = t_st

                    closures = [mk_group(tcc, nh)
                                for tcc in range(4) for nh in range(2)]
                    if flush:
                        for fn in closures:
                            fn()
                        for fn in (ln3_a, ln3_b, ln3_c, ln3_d, ln3_e, ln3_f,
                                   ln4_a, ln4_b, ln4_c, ln4_d):
                            fn()
                    else:
                        gp0 = (ic + 1) * 4
                        slots = [(0, 21), (0, 24), (0, 27), (1, 3), (1, 7),
                                 (1, 13), (1, 21), (1, 27)]
                        for fn, (dp, jt) in zip(closures, slots):
                            sched_add(gp0 + dp, jt, fn)
                        for fn, jt in zip((ln3_a, ln3_b, ln3_c, ln3_d, ln3_e,
                                           ln3_f),
                                          (4, 7, 10, 13, 17, 22)):
                            sched_add(gp0 + 2, jt, fn)
                        for fn, jt in zip((ln4_a, ln4_b, ln4_c, ln4_d),
                                          (4, 8, 12, 17)):
                            sched_add(gp0 + 3, jt, fn)

                fprep0_on = (n_ic == 8 and n_pair == 4 and phase_limit >= 3
                             and n_pass > 0)
                fprep0_sched = {(4, 2): 0, (4, 3): 1, (5, 0): 2, (5, 1): 3,
                                (5, 2): 4, (5, 3): 5, (6, 0): 6, (6, 1): 7}
                for ic in range(n_ic if phase_limit >= 2 else 0):
                    i0 = ic * 512
                    if ic == 4 and fprep0_on:
                        tok_x2p0[2] = tok_x2p0[4] = ((dS, dS.n, P.gidx),
                                                     (dS2, dS2.n, P.gidx))
                    for pair in range(n_pair):
                        tok_exp = {}
                        tok_pv = None

                        def emit_pv(jp):
                            nonlocal tok_pv
                            pp4 = jp % 4
                            P.do("tensor", lambda e, pp4=pp4, jp=jp: e.matmul(
                                O_A[:], v_ones[:, jp * 65:(jp + 1) * 65],
                                pT[pp4][:, 0:512],
                                start=(jp == 0), stop=(jp == 31),
                                skip_group_check=True),
                                waits=(tok_exp[jp], o_free) if jp == 0 else
                                      (tok_exp[jp],))
                            tok_pv = P.do("tensor", lambda e, pp4=pp4, jp=jp:
                                          e.matmul(
                                O_B[:], v_ones[:, jp * 65:(jp + 1) * 65],
                                pT[pp4][:, 512:1024],
                                start=(jp == 0), stop=(jp == 31),
                                skip_group_check=True), inc=sT)
                            pT_free[pp4] = tok_pv

                        for jt in range(32):
                            p2 = jt % 2
                            p4 = jt % 4
                            j0 = jt * 128
                            P.do("tensor", lambda e, p2=p2, j0=j0, i0=i0, pair=pair:
                                 e.matmul(S_AB[p2][:, 0:512],
                                          k_fm2[0:64, j0:j0 + 128],
                                          q_fm[pair][0:64, i0:i0 + 512],
                                          start=True, stop=True,
                                          tile_position=(0, 0),
                                          skip_group_check=True),
                                 waits=(sab_free[p2],))
                            t_qk = P.do("tensor", lambda e, p2=p2, j0=j0, i0=i0,
                                        pair=pair:
                                        e.matmul(S_AB[p2][:, 512:1024],
                                                 k_fm2[64:128, j0:j0 + 128],
                                                 q_fm[pair][64:128, i0:i0 + 512],
                                                 start=True, stop=True,
                                                 tile_position=(64, 0),
                                                 skip_group_check=True),
                                        inc=sT)
                            t_e = P.do("scalar", lambda e, p2=p2, p4=p4: e.activation(
                                pT[p4][:], S_AB[p2][:], AF.Exp, bias=neg16),
                                waits=(t_qk, pT_free[p4]), inc=sS)
                            tok_exp[jt] = t_e
                            sab_free[p2] = t_e
                            gp = ic * 4 + pair
                            for fn in sched.pop((gp, jt), ()):
                                fn()
                            if jt == 10 and pending_norm[0]:
                                pending_norm[0].pop(0)()
                            if jt == 18 and pending_norm[0]:
                                norm_last[0] = pending_norm[0].pop(0)()
                            if jt >= 2:
                                emit_pv(jt - 2)
                        emit_pv(30)
                        emit_pv(31)
                        pp2 = pair % 2
                        tva = P.do("vector", lambda e, pp2=pp2: e.tensor_copy(
                            oa_sb[pp2][:], O_A[:]),
                            waits=(tok_pv, oa_free[pp2]), inc=sV)
                        tvb = P.do("vector", lambda e, pp2=pp2: e.tensor_copy(
                            ob_sb[pp2][:], O_B[:]), inc=sV)
                        o_free = tvb
                        t_rc = emit_norm_recip(ic, pair, tva, tvb)
                        pending_norm[0] = [make_norm_bc(ic, pair, t_rc, 0),
                                           make_norm_bc(ic, pair, t_rc, 1)]
                        if (ic, pair) in fprep0_sched and fprep0_on:
                            gpn = ic * 4 + pair + 1
                            for fn, jtp in zip(
                                    mk_fprep0_pieces(fprep0_sched[(ic, pair)]),
                                    (20, 23, 26, 29)):
                                sched_add(gpn, jtp, fn)
                    if ic == n_ic - 1 and pending_norm[0]:
                        pending_norm[0].pop(0)()
                        norm_last[0] = pending_norm[0].pop(0)()
                    if n_pair == 4:
                        emit_phase4(ic, flush=(ic == n_ic - 1))
                if phase_limit >= 2 and n_ic > 0 and n_pair == 4:
                    for key in sorted(sched):
                        for fn in sched.pop(key):
                            fn()

                tok_x2_done = (dS, dS.n, P.gidx)
                tok_x2_done2 = (dS2, dS2.n, P.gidx)

        P.barrier(bar)

        with ExitStack() as fscope:
            fsb = lambda name, shape, dt: fscope.enter_context(nc.sbuf_tensor(name, shape, dt))
            fps = lambda name, shape, dt: fscope.enter_context(nc.psum_tensor(name, shape, dt))

            f_fm = fsb("f_fm", [128, 8 * TT], BF16)
            h_fm = fsb("h_fm", [128, 32 * TT], BF16)
            u1_fm = fsb("u1_fm", [128, 32 * TT], BF16)
            w12s = [fsb(f"w12s{i}", [128, 2048], BF16) for i in range(2)]
            h1s = [fsb(f"h1s{i}", [128, 4096], BF16) for i in range(2)]
            sil = [fsb(f"sil{i}", [128, 512], F32) for i in range(2)]
            oscb = fsb("oscb", [128, 2048], BF16)
            osc = [oscb[:, i * 512:(i + 1) * 512] for i in range(4)]
            xr2 = fsb("xr2", [128, D], F32)
            xr3 = xr2
            f_bf = fbf0
            ffn_tm = f_bf
            outt = xb0[0]
            sq5 = xb0[1]
            stats5 = fsb("stats5", [128, 8], F32)

            psG1 = [PSf[:, 0:512], PSf[:, 512:1024]]
            psG2 = [PSf[:, 1024:1536], PSf[:, 1536:2048]]
            psU = [PSf[:, 2048:2560], PSf[:, 2560:3072]]
            trf_ps = PSb[:, 6 * 1024:7 * 1024]
            trf_ps2 = PSb[:, 7 * 1024:8 * 1024]
            psT_par = [PSb[:, 7 * 1024:7 * 1024 + 128],
                       PSb[:, 6 * 1024:6 * 1024 + 128],
                       PSb[:, 3 * 1024:3 * 1024 + 128],
                       PSb[:, 2 * 1024:2 * 1024 + 128]]

            w1s_free = [None, None]
            w2s_free = [None, None]
            psG1_free = [None, None]
            h1s_free = [None, None]
            psG_free = [None, None]
            psU_free = [None, None]
            sil_free = [None, None]
            osc_free = [None, None]
            trf_free = [None, None]
            pst_free = [None, None, None, None]
            xr2_free = None
            xr3_free = None
            sq5_free = None
            fbf_free = [None] * 8
            t_hfm_lastread = None
            t_ffm_lastread = None
            t_u1_lastread = None
            x2_ready = tok_x2_done
            x2_ready2 = tok_x2_done2
            outt_free = None
            ffntm_free = None
            tap_tok_hfm = None
            tap_tok_u1fm = None
            tap_tok_ffntm = None

            fp_toks = []
            fprep_v_half = [None, None]
            fprep_state = {"ffntm_free": None}
            ln6_q = []

            def emit_fprep_chain(pz, tcc):
                nonlocal xr2_free, sq5_free, outt_free
                if tcc == 0:
                    fp_toks.clear()
                t0 = pz * TT
                nm = stats5[:, 0:1]
                vr = stats5[:, 1:2]
                yy = stats5[:, 2:3]
                tt_ = stats5[:, 3:4]
                if True:
                    r0 = t0 + tcc * 128
                    if tcc % 2 == 0:
                        src = xr2
                        w_src = (xr2_free, xr3_free)
                        sem = dX[0]
                    else:
                        src = sq5
                        w_src = (sq5_free,)
                        sem = dW2[0]
                    dst = f_bf[:, tcc * D:(tcc + 1) * D]
                    tok_xl = P.do("sync", lambda e, r0=r0, src=src: e.dma_start(
                        out=src[:], in_=x2s_d[r0:r0 + 128, :]),
                        waits=(x2_ready, x2_ready2) + w_src, inc=sem, dma=True)
                    t_r = P.do("vector", lambda e, src=src: e.reduce_sum(
                        nm, src[:], AX.X), waits=(tok_xl,), inc=sV)
                    t_nm = P.do("vector", lambda e: e.tensor_scalar_mul(
                        nm, nm, -1.0 / D), waits=(t_r,), inc=sV)
                    t_c = P.do("vector", lambda e, src=src: e.tensor_scalar_add(
                        src[:], src[:], nm), waits=(t_nm,), inc=sV)
                    t_q = P.do("vector", lambda e, src=src: e.tensor_mul(
                        outt[:], src[:], src[:]), waits=(t_c, outt_free),
                        inc=sV)
                    t_s = P.do("vector", lambda e: e.reduce_sum(
                        vr, outt[:], AX.X), waits=(t_q,), inc=sV)
                    outt_free = t_s
                    t_ve = P.do("vector", lambda e: e.tensor_scalar(
                        vr, vr, 1.0 / D, eps_t, ALU.mult, ALU.add),
                        waits=(t_s,), inc=sV)
                    t_rs = rsqrt_v(vr, yy, tt_, (t_ve,))
                    t_f = P.do("vector", lambda e, src=src, dst=dst:
                               e.tensor_scalar_mul(dst, src[:], yy),
                        waits=(t_rs, fbf_free[tcc], last_tap[0]), inc=sV)
                    if tcc % 2 == 0:
                        xr2_free = t_f
                    else:
                        sq5_free = t_f
                    fp_toks.append(t_f)

            def emit_fprep_ln(pz):
                for tcc in range(8):
                    emit_fprep_chain(pz, tcc)

            def emit_fprep_ln0():
                nonlocal xr2_free, sq5_free, outt_free
                fp_toks.clear()
                C = {}

                def sl(t):
                    b = 4 * (t % 2)
                    return (stats5[:, b:b + 1], stats5[:, b + 1:b + 2],
                            stats5[:, b + 2:b + 3], stats5[:, b + 3:b + 4])

                def srcb(t):
                    return xr2 if t % 2 == 0 else sq5

                def c_load(t):
                    nonlocal xr2_free, sq5_free
                    r0 = t * 128
                    w_src = (xr2_free,) if t % 2 == 0 else (sq5_free,)
                    sem = dX[0] if t % 2 == 0 else dW2[0]
                    C[t, "xl"] = P.do("sync", lambda e, r0=r0, t=t: e.dma_start(
                        out=srcb(t)[:], in_=x2s_d[r0:r0 + 128, :]),
                        waits=(x2_ready, x2_ready2) + w_src, inc=sem, dma=True)

                def c_red(t):
                    nm = sl(t)[0]
                    t_r = P.do("vector", lambda e, nm=nm, t=t: e.reduce_sum(
                        nm, srcb(t)[:], AX.X), waits=(C[t, "xl"],), inc=sV)
                    C[t, "nm"] = P.do("vector", lambda e, nm=nm:
                                      e.tensor_scalar_mul(nm, nm, -1.0 / D),
                                      waits=(t_r,), inc=sV)

                def c_sq(t):
                    nm = sl(t)[0]
                    C[t, "sq"] = P.do("scalar", lambda e, nm=nm, t=t:
                                      e.activation(
                        outt[:], srcb(t)[:], AF.Square, bias=nm),
                        waits=(C[t, "nm"], C.get((t - 1, "s2")), outt_free),
                        inc=sS)

                def c_mid(t):
                    nm, vr, sd, rstd = sl(t)
                    t_s = P.do("vector", lambda e, vr=vr: e.reduce_sum(
                        vr, outt[:], AX.X), waits=(C[t, "sq"],), inc=sV)
                    C[t, "s2"] = t_s
                    C[t, "sd"] = P.do("scalar", lambda e, vr=vr, sd=sd:
                                      e.activation(
                        sd, vr, AF.Sqrt, bias=eps_t, scale=1.0 / D),
                        waits=(t_s,), inc=sS)

                def c_ap(t):
                    nonlocal xr2_free, sq5_free, outt_free
                    nm, vr, sd, rstd = sl(t)
                    dst = f_bf[:, t * D:(t + 1) * D]
                    t_rs = P.do("vector", lambda e, sd=sd, rstd=rstd:
                                e.reciprocal(rstd, sd),
                                waits=(C[t, "sd"],), inc=sV)
                    t_f = P.do("vector", lambda e, nm=nm, rstd=rstd, dst=dst,
                               t=t: e.tensor_scalar(
                        dst, srcb(t)[:], nm, rstd, ALU.add, ALU.mult),
                        waits=(t_rs, fbf_free[t], last_tap[0]), inc=sV)
                    if t % 2 == 0:
                        xr2_free = t_f
                    else:
                        sq5_free = t_f
                    outt_free = C[t, "s2"]
                    fp_toks.append(t_f)

                c_load(0)
                c_red(0)
                c_sq(0)
                for t in range(8):
                    c_mid(t)
                    if t + 1 < 8:
                        c_load(t + 1)
                        c_red(t + 1)
                    c_ap(t)
                    if t + 1 < 8:
                        c_sq(t + 1)

            def emit_fprep_tr(pz, src_bf=None, toks=None):
                nonlocal tok_fprep_v
                sbf = f_bf if src_bf is None else src_bf
                tks = fp_toks if toks is None else toks
                for tcc in range(8):
                    tb = tcc % 2
                    tps = trf_ps if tb == 0 else trf_ps2
                    ttr = None
                    for dc in range(8):
                        ttr = P.do("tensor", lambda e, dc=dc, tcc=tcc, tps=tps,
                                   sbf=sbf: e.transpose(
                            tps[:, dc * 128:(dc + 1) * 128],
                            sbf[:, tcc * D + dc * 128:tcc * D + dc * 128 + 128],
                            ident_bf[:]),
                            waits=(tks[tcc], trf_free[tb], t_ffm_lastread,
                                   pst_free[1 - tb]) if dc == 0 else (),
                            inc=sT if dc == 7 else None)
                    fbf_free[tcc] = ttr
                    tcp = P.do("scalar", lambda e, tcc=tcc, tps=tps:
                               e.activation(
                        f_fm.rearrange("p (c t) -> p c t", t=TT)
                            [:, :, tcc * 128:(tcc + 1) * 128],
                        tps[:].rearrange("p (c w) -> p c w", w=128),
                        AF.Copy), waits=(ttr,), inc=sS)
                    trf_free[tb] = tcp
                    tok_fprep_v = tcp
                    if tcc == 3:
                        fprep_v_half[0] = tcp
                    elif tcc == 7:
                        fprep_v_half[1] = tcp
                pst_free[1] = trf_free[0]
                pst_free[0] = trf_free[1]
                fprep_state["ffntm_free"] = (sT, sT.n, P.gidx)

            tok_fprep_v = None
            if phase_limit >= 3 and n_pass > 0:
                if fp0_toks[7] is not None:
                    emit_fprep_tr(0, src_bf=fbf0, toks=fp0_toks)
                else:
                    emit_fprep_ln0()
                    emit_fprep_tr(0)

            for pz in range(n_pass if phase_limit >= 3 else 0):
                t0 = pz * TT
                t_hlast = None
                t_glast_mm = None
                h1_pre = {}
                for jc in range(32 if ffn_stage >= 2 else 0):
                    if ln6_q and jc % 2 == 0:
                        ln6_q.pop(0)()
                    if jc in (26, 29) and ffn_stage >= 3:
                        oc0 = 0 if jc == 26 else 1
                        h1_pre[oc0] = P.do("gpsimd", lambda e, oc0=oc0:
                                           e.dma_start(
                            out=h1s[oc0][:],
                            in_=h1_d[oc0].rearrange("p a b -> p (a b)")),
                            waits=(h1s_free[oc0],), inc=dWH[oc0], dma=True)
                    pj = jc % 2
                    tok_w1 = tok_w2 = P.do("gpsimd", lambda e, jc=jc, pj=pj:
                                           e.dma_start(
                        out=w12s[pj][:],
                        in_=w12_d[jc].rearrange("p a b -> p (a b)")),
                        waits=(w1s_free[pj],), inc=dW1[pj], dma=True)
                    for nch in range(2):
                        pb = nch
                        pst_h = pst_free[3 - pb] if jc == 0 else None
                        tm1 = None
                        for dc in range(8):
                            w0 = ()
                            if dc == 0:
                                w0 = (tok_w1, psG_free[pb], psG1_free[pb],
                                      fprep_v_half[nch], tok_scales,
                                      t_hfm_lastread)
                            tm1 = P.do("tensor", lambda e, dc=dc, pj=pj, nch=nch, pb=pb:
                                       e.matmul(
                                psG1[pb][:], w12s[pj][:, dc * 128:(dc + 1) * 128],
                                f_fm[:, dc * TT + nch * 512:dc * TT + nch * 512 + 512],
                                start=(dc == 0), stop=(dc == 7),
                                skip_group_check=True),
                                waits=w0, inc=sT if dc == 7 else None)
                        tm2 = None
                        for dc in range(8):
                            w0 = (tok_w2, pst_h) if dc == 0 else ()
                            tm2 = P.do("tensor", lambda e, dc=dc, pj=pj, nch=nch, pb=pb:
                                       e.matmul(
                                psG2[pb][:],
                                w12s[pj][:, 1024 + dc * 128:1024 + (dc + 1) * 128],
                                f_fm[:, dc * TT + nch * 512:dc * TT + nch * 512 + 512],
                                start=(dc == 0), stop=(dc == 7),
                                skip_group_check=True),
                                waits=w0, inc=sT if dc == 7 else None)
                        if nch == 1:
                            w1s_free[pj] = tm2
                            t_glast_mm = tm2
                        t_sil = P.do("scalar", lambda e, pb=pb, jc=jc: e.activation(
                            sil[pb][:], psG1[pb][:], AF.Silu,
                            scale=s1c[:, jc:jc + 1]),
                            waits=(tm1, sil_free[pb]), inc=sS)
                        psG1_free[pb] = t_sil
                        t_h = P.do("vector", lambda e, pb=pb, jc=jc, nch=nch:
                                   e.scalar_tensor_tensor(
                            h_fm[:, jc * TT + nch * 512:jc * TT + nch * 512 + 512],
                            psG2[pb][:], s2c[:, jc:jc + 1], sil[pb][:],
                            ALU.mult, ALU.mult),
                            waits=(tm2, t_sil), inc=sV)
                        psG_free[pb] = t_h
                        sil_free[pb] = t_h
                        t_hlast = t_h
                t_ffm_lastread = t_glast_mm
                if pz + 1 < n_pass and ffn_stage < 3:
                    emit_fprep_ln(pz + 1)
                if "hfm" in tap_d and pz == 0:
                    tap_tok_hfm = last_tap[0] = P.do("gpsimd", lambda e: e.dma_start(
                        out=tap_d["hfm"], in_=h_fm[:]),
                        waits=(t_hlast, last_tap[0]), inc=dT, dma=True)

                t_u1last = None
                tu1mm_last = None
                for oc in range(32 if ffn_stage >= 3 else 0):
                    if ln6_q and oc < 16:
                        ln6_q.pop(0)()
                    if pz + 1 < n_pass and oc >= 16 and oc % 2 == 0:
                        emit_fprep_chain(pz + 1, (oc - 16) // 2)
                    po = oc % 2
                    if oc in h1_pre:
                        tok_h1 = h1_pre[oc]
                    else:
                        tok_h1 = P.do("gpsimd", lambda e, oc=oc, po=po:
                                      e.dma_start(
                            out=h1s[po][:],
                            in_=h1_d[oc].rearrange("p a b -> p (a b)")),
                            waits=(h1s_free[po],), inc=dWH[po], dma=True)
                    for nch in range(2):
                        pb = nch
                        tmm = None
                        for jc in range(32):
                            w0 = ()
                            if jc == 0:
                                w0 = (tok_h1, psU_free[pb], t_hlast)
                            tmm = P.do("tensor", lambda e, jc=jc, po=po, nch=nch,
                                       pb=pb: e.matmul(
                                psU[pb][:], h1s[po][:, jc * 128:(jc + 1) * 128],
                                h_fm[:, jc * TT + nch * 512:jc * TT + nch * 512 + 512],
                                start=(jc == 0), stop=(jc == 31),
                                skip_group_check=True),
                                waits=w0, inc=sT if jc == 31 else None)
                        if nch == 1:
                            h1s_free[po] = tmm
                        tu1mm_last = tmm
                        t_u1 = P.do("scalar", lambda e, pb=pb, oc=oc, nch=nch:
                                    e.activation(
                            u1_fm[:, oc * TT + nch * 512:oc * TT + nch * 512 + 512],
                            psU[pb][:], AF.Gelu, scale=sh1c[:, oc:oc + 1]),
                            waits=(tmm, t_u1_lastread, tap_tok_u1fm), inc=sS)
                        psU_free[pb] = t_u1
                        t_u1last = t_u1
                t_hfm_u1read = tu1mm_last
                if pz + 1 < n_pass:
                    emit_fprep_tr(pz + 1)

                t_u2last = None
                tu2mm_last = None
                for oc in range(32 if ffn_stage >= 4 else 0):
                    po = oc % 2
                    tok_h2 = P.do("gpsimd", lambda e, oc=oc, po=po: e.dma_start(
                        out=h1s[po][:], in_=h2_d[oc].rearrange("p a b -> p (a b)")),
                        waits=(h1s_free[po],), inc=dWH[po], dma=True)
                    for nch in range(2):
                        pb = nch
                        tmm = None
                        for jc in range(32):
                            w0 = ()
                            if jc == 0:
                                w0 = (tok_h2, psU_free[pb], t_u1last, t_hfm_u1read)
                            tmm = P.do("tensor", lambda e, jc=jc, po=po, nch=nch,
                                       pb=pb: e.matmul(
                                psU[pb][:], h1s[po][:, jc * 128:(jc + 1) * 128],
                                u1_fm[:, jc * TT + nch * 512:jc * TT + nch * 512 + 512],
                                start=(jc == 0), stop=(jc == 31),
                                skip_group_check=True),
                                waits=w0, inc=sT if jc == 31 else None)
                        if nch == 1:
                            h1s_free[po] = tmm
                        tu2mm_last = tmm
                        t_u2 = P.do("scalar", lambda e, pb=pb, oc=oc, nch=nch:
                                    e.activation(
                            h_fm[:, oc * TT + nch * 512:oc * TT + nch * 512 + 512],
                            psU[pb][:], AF.Gelu, scale=sh2c[:, oc:oc + 1]),
                            waits=(tmm, tap_tok_hfm), inc=sS)
                        psU_free[pb] = t_u2
                        t_u2last = t_u2
                t_u1_lastread = tu2mm_last
                if "u1fm" in tap_d and pz == 0:
                    tap_tok_u1fm = last_tap[0] = P.do("gpsimd", lambda e: e.dma_start(
                        out=tap_d["u1fm"], in_=u1_fm[:]),
                        waits=(t_u1last, last_tap[0]), inc=dT, dma=True)

                psU4 = [psU[0], psU[1], psG1[0], psG1[1]]
                psU4_free = [psU_free[0], psU_free[1],
                             psG1_free[0], psG1_free[1]]
                osc4_free = [osc_free[0], osc_free[1], None, None]
                t_w3_lastread = None
                tlast = None
                pend_tr = [None]
                ln6_tok = [None] * 8
                for dc in range(8 if ffn_stage >= 5 else 0):
                    pw = dc % 2
                    tok_w3 = P.do("gpsimd", lambda e, dc=dc, pw=pw: e.dma_start(
                        out=h1s[pw][:],
                        in_=w3_d[dc].rearrange("p a b -> p (a b)")),
                        waits=(h1s_free[pw],), inc=dWH[pw], dma=True)
                    for nch in range(2):
                        pb = (dc * 2 + nch) % 4
                        tmm = None
                        for jc in range(32):
                            w0 = ()
                            if jc == 0:
                                w0 = (tok_w3, psU4_free[pb], t_u2last)
                            tmm = P.do("tensor", lambda e, jc=jc, pw=pw, nch=nch,
                                       pb=pb: e.matmul(
                                psU4[pb][:],
                                h1s[pw][:, jc * 128:(jc + 1) * 128],
                                h_fm[:, jc * TT + nch * 512:jc * TT + nch * 512 + 512],
                                start=(jc == 0), stop=(jc == 31),
                                skip_group_check=True),
                                waits=w0, inc=sT if jc == 31 else None)
                        if nch == 1:
                            h1s_free[pw] = tmm
                        t_w3_lastread = tmm
                        t_sc3 = P.do("vector", lambda e, pb=pb, dc=dc:
                                     e.tensor_scalar_mul(
                            osc[pb][:], psU4[pb][:], s3c[:, dc:dc + 1]),
                            waits=(tmm, osc4_free[pb]), inc=sV)
                        psU4_free[pb] = t_sc3

                        def mk_tr(pb=pb, dc=dc, nch=nch, t_sc3=t_sc3):
                            nonlocal tlast
                            tl2 = None
                            for qq in range(4):
                                pt = qq
                                tcc = nch * 4 + qq
                                ttr = P.do("tensor", lambda e, pb=pb, qq=qq,
                                           pt=pt: e.transpose(
                                    psT_par[pt][:],
                                    osc[pb][:, qq * 128:(qq + 1) * 128],
                                    ident_bf[:]),
                                    waits=(t_sc3, pst_free[pt]), inc=sT)
                                tcp = P.do("vector", lambda e, tcc=tcc, dc=dc,
                                           pt=pt: e.tensor_copy(
                                    ffn_tm[:, tcc * D + dc * 128:
                                           tcc * D + dc * 128 + 128],
                                    psT_par[pt][:]),
                                    waits=(ttr,
                                           fprep_state["ffntm_free"]
                                           if (dc == 0 and qq == 0) else None,
                                           fbf_free[tcc] if dc == 0 else None),
                                    inc=sV)
                                pst_free[pt] = tcp
                                tl2 = tcp
                            osc4_free[pb] = tl2
                            tlast = tl2

                        if pend_tr[0] is not None:
                            pend_tr[0]()
                        pend_tr[0] = mk_tr
                if pend_tr[0] is not None:
                    pend_tr[0]()
                    pend_tr[0] = None
                tok_alltm = (sV, sV.n, P.gidx)

                def mk_ln6(tcc, t0=t0, tok_alltm=tok_alltm):
                    st = {}
                    src_ = ffn_tm[:, tcc * D:(tcc + 1) * D]
                    nm = stats5[:, 4:5]
                    vr = stats5[:, 5:6]
                    yy = stats5[:, 6:7]
                    tt_ = stats5[:, 7:8]
                    r0 = t0 + tcc * 128

                    def p1():
                        nonlocal xr3_free
                        st["xl"] = P.do("sync", lambda e: e.dma_start(
                            out=xr3[:], in_=x2s_d[r0:r0 + 128, :]),
                            waits=(xr3_free, xr2_free, x2_ready, x2_ready2),
                            inc=dX[1], dma=True)
                        t_r = P.do("vector", lambda e: e.reduce_sum(
                            nm, src_, AX.X), waits=(tok_alltm,), inc=sV)
                        st["nm"] = P.do("vector", lambda e: e.tensor_scalar_mul(
                            nm, nm, -1.0 / D), waits=(t_r,), inc=sV)

                    def p2():
                        nonlocal sq5_free
                        t_c = P.do("vector", lambda e: e.tensor_scalar_add(
                            outt[:], src_, nm), waits=(st["nm"], outt_free,
                                                       sq5_free),
                            inc=sV)
                        t_q = P.do("vector", lambda e: e.tensor_mul(
                            sq5[:], outt[:], outt[:]), waits=(t_c,), inc=sV)
                        t_s = P.do("vector", lambda e: e.reduce_sum(
                            vr, sq5[:], AX.X), waits=(t_q,), inc=sV)
                        sq5_free = t_s
                        st["ve"] = P.do("vector", lambda e: e.tensor_scalar(
                            vr, vr, 1.0 / D, eps_t, ALU.mult, ALU.add),
                            waits=(t_s,), inc=sV)

                    def p3():
                        st["rs"] = rsqrt_v(vr, yy, tt_, (st["ve"],))

                    def p4():
                        nonlocal xr3_free, outt_free
                        t_o = P.do("vector", lambda e: e.scalar_tensor_tensor(
                            outt[:], outt[:], yy, xr3[:], ALU.mult, ALU.add),
                            waits=(st["rs"], st["xl"]), inc=sV)
                        fbf_free[tcc] = t_o
                        xr3_free = t_o
                        t_so = P.do("sync", lambda e: e.dma_start(
                            out=out_d[r0:r0 + 128, :], in_=outt[:]),
                            waits=(t_o,), inc=dS, dma=True)
                        outt_free = t_so
                    return [p1, p2, p3, p4]

                def emit_ln6_final(t0=t0):
                    nonlocal xr3_free, xr2_free, outt_free, sq5_free
                    C = {}
                    w12f = [w12s[0].ap().bitcast(F32),
                            w12s[1].ap().bitcast(F32)]
                    res_b = [outt, w12f[0]]
                    src_b = [xr3, w12f[1]]
                    resb_free = [outt_free, w1s_free[0]]
                    srcb_free = [xr3_free, w1s_free[1]]

                    def sl(t):
                        g = t % 2
                        b = 4 * g
                        return (stats5[:, b:b + 1], stats5[:, b + 1:b + 2],
                                stats5[:, b + 2:b + 3], stats5[:, b + 3:b + 4])

                    def c_load(t):
                        r0 = t0 + t * 128
                        b = t % 2
                        C[t, "xl"] = P.do("sync", lambda e, r0=r0, b=b:
                                          e.dma_start(
                            out=src_b[b][:], in_=x2s_d[r0:r0 + 128, :]),
                            waits=(srcb_free[b],
                                   xr2_free if t == 0 else None,
                                   x2_ready, x2_ready2),
                            inc=dX[b], dma=True)

                    def c_red(t):
                        nm, vr, yy, tt_ = sl(t)
                        src_ = ffn_tm[:, t * D:(t + 1) * D]
                        t_r = P.do("vector", lambda e, nm=nm, src_=src_:
                                   e.reduce_sum(nm, src_, AX.X),
                                   waits=(tok_alltm,), inc=sV)
                        C[t, "nm"] = P.do("vector", lambda e, nm=nm:
                                          e.tensor_scalar_mul(nm, nm, -1.0 / D),
                                          waits=(t_r,), inc=sV)

                    def c_sq(t):
                        nm, vr, yy, tt_ = sl(t)
                        src_ = ffn_tm[:, t * D:(t + 1) * D]
                        C[t, "sq"] = P.do("scalar", lambda e, nm=nm, src_=src_:
                                          e.activation(
                            sq5[:], src_, AF.Square, bias=nm),
                            waits=(C[t, "nm"], C.get((t - 1, "s2")), sq5_free),
                            inc=sS)

                    def c_mid(t):
                        nm, vr, yy, tt_ = sl(t)
                        t_s = P.do("vector", lambda e, vr=vr: e.reduce_sum(
                            vr, sq5[:], AX.X), waits=(C[t, "sq"],), inc=sV)
                        C[t, "s2"] = t_s
                        t_ve = P.do("vector", lambda e, vr=vr: e.tensor_scalar(
                            vr, vr, 1.0 / D, eps_t, ALU.mult, ALU.add),
                            waits=(t_s,), inc=sV)
                        C[t, "rs"] = rsqrt_v(vr, yy, tt_, (t_ve,))

                    def c_out(t):
                        nm, vr, yy, tt_ = sl(t)
                        src_ = ffn_tm[:, t * D:(t + 1) * D]
                        r0 = t0 + t * 128
                        b = t % 2
                        t1 = P.do("vector", lambda e, nm=nm, yy=yy, src_=src_,
                                  b=b: e.tensor_scalar(
                            res_b[b][:], src_, nm, yy, ALU.add, ALU.mult),
                            waits=(C[t, "rs"], resb_free[b]), inc=sV)
                        t2 = P.do("vector", lambda e, b=b: e.tensor_tensor(
                            res_b[b][:], res_b[b][:], src_b[b][:], ALU.add),
                            waits=(t1, C[t, "xl"]), inc=sV)
                        fbf_free[t] = t2
                        srcb_free[b] = t2
                        t_so = P.do("sync", lambda e, r0=r0, b=b: e.dma_start(
                            out=out_d[r0:r0 + 128, :], in_=res_b[b][:]),
                            waits=(t2,), inc=dS if b == 0 else dS2, dma=True)
                        resb_free[b] = t_so

                    c_load(0)
                    c_red(0)
                    c_sq(0)
                    for t in range(8):
                        c_mid(t)
                        if t + 1 < 8:
                            c_red(t + 1)
                        c_out(t)
                        if t + 1 < 8:
                            c_load(t + 1)
                            c_sq(t + 1)

                if ffn_stage >= 5:
                    if pz == n_pass - 1 and n_pass == NPASS:
                        emit_ln6_final()
                    else:
                        for tcc in range(8):
                            ln6_q.extend(mk_ln6(tcc))
                if pz == n_pass - 1:
                    while ln6_q:
                        ln6_q.pop(0)()
                t_hfm_lastread = t_w3_lastread
                if ffn_stage >= 5:
                    psU_free[0] = psU4_free[0]
                    psU_free[1] = psU4_free[1]
                    psG1_free[0] = psU4_free[2]
                    psG1_free[1] = psU4_free[3]
                    psG_free[0] = psU4_free[2]
                    psG_free[1] = psU4_free[3]
                    osc_free[0] = osc4_free[0]
                    osc_free[1] = osc4_free[1]
                if "ffntm" in tap_d and pz == 0:
                    tap_tok_ffntm = last_tap[0] = P.do("gpsimd", lambda e: e.dma_start(
                        out=tap_d["ffntm"], in_=ffn_tm[:, 6 * D:8 * D]),
                        waits=((sV, sV.n, P.gidx), last_tap[0]), inc=dT, dma=True)
                    fbf_free[6] = tap_tok_ffntm
                    fbf_free[7] = tap_tok_ffntm

            P.do("vector", None, waits=((dS, dS.n),))
            P.do("gpsimd", None, waits=((dS, dS.n), (dS2, dS2.n), (dC, dC.n), (dT, dT.n)))

        P.emit()
    return nc



def _stage(inputs):
    f = np.ascontiguousarray
    return {
        "wq": f(inputs["L0_attn_Wq"]),
        "wkv": f(inputs["L0_attn_Wkv"]),
        "wout": f(inputs["L0_attn_Wout"]),
        "gq": f(inputs["L0_attn_norm_g"]),
        "go": f(inputs["L0_attn_out_norm_g"]),
        "w12": f(np.concatenate(
            [inputs["L0_ffn_w1_w"].reshape(32, 128, 8, 128).transpose(0, 3, 2, 1),
             inputs["L0_ffn_w2_w"].reshape(32, 128, 8, 128).transpose(0, 3, 2, 1)],
            axis=2)),
        "h1": f(inputs["L0_ffn_h1_w"].reshape(32, 128, 32, 128).transpose(0, 3, 2, 1)),
        "h2": f(inputs["L0_ffn_h2_w"].reshape(32, 128, 32, 128).transpose(0, 3, 2, 1)),
        "w3": f(inputs["L0_ffn_w3_w"].reshape(8, 128, 32, 128).transpose(0, 3, 2, 1)),
        "s1": f(inputs["L0_ffn_w1_s"]),
        "s2": f(inputs["L0_ffn_w2_s"]),
        "sh1": f(inputs["L0_ffn_h1_s"]),
        "sh2": f(inputs["L0_ffn_h2_s"]),
        "s3": f(inputs["L0_ffn_w3_s"]),
    }


def kernel(**inputs):
    np_inputs = {k: np.asarray(v, dtype=np.float32) for k, v in inputs.items()}
    x = np_inputs["L0_attn_x"]
    shared = _stage(np_inputs)
    nc = build(TAPS)
    in_maps = []
    for b in range(8):
        m = dict(shared)
        m["x"] = np.ascontiguousarray(x[b])
        in_maps.append(m)
    res = run_bass_kernel_spmd(nc, in_maps, core_ids=list(range(8)))
    out = np.stack([res.results[b]["out"] for b in range(8)], axis=0)
    if TAPS:
        kernel.tap_results = res.results
    return out
